# version 24
# baseline (speedup 1.0000x reference)
import math
from contextlib import ExitStack

import numpy as np

import concourse.bass as bass
import concourse.mybir as mybir
from concourse.bass_utils import run_bass_kernel_spmd

F32 = mybir.dt.float32
BF16 = mybir.dt.bfloat16
I8 = mybir.dt.int8
AF = mybir.ActivationFunctionType
ALU = mybir.AluOpType
AXX = mybir.AxisListType.X

B, S, H = 4, 4096, 2048
NH = 16
INTER = 8192
OWN = 2048
TA = 512
NT_A = 8
NT_A_RUN = 4
TC = 512
NT_C = 4
BSUB = 4
TWO_PI = 6.283185307179586
PI = 3.141592653589793
HALF_PI = 1.5707963267948966
NRED = 17
ATTN_SCALE = 1.0 / math.sqrt(128.0)

XS2 = ("dsl2", "dk1", "dk2", "dk3")
ENGINES = ("sync", "gpsimd", "tensor", "vector", "scalar")
DMA_SEMS = ("dI", "dIb", "dwdkv", "dwdq", "dwuv", "dwuqr", "dwuk", "dwkr",
            "dx0", "dx1", "dx2", "dx3", "dk0", "dk1", "dk2", "dk3", "dwo0", "dwo1", "dwo2", "dwo3", "dsl0", "dsl1", "dsl2",
            "dwd0", "dwd1", "dwd2", "dwd3", "dy0", "dy1", "dmc", "scc")
SEMS = ("sp", "sv", "sa", "sg") + DMA_SEMS
SEM_OWNER = {"sp": "tensor", "sv": "vector", "sa": "scalar", "sg": "gpsimd"}


class Sched:
    def __init__(self):
        self.per_engine = {e: [] for e in ENGINES}
        self.counts = {s: 0 for s in SEMS}
        self.hwm = {e: {s: 0 for s in SEMS} for e in ENGINES}

    OWN = {"vector": "sv", "scalar": "sa", "gpsimd": "sg"}

    def op(self, eng, fn, inc=None, incval=1, waits=()):
        ws = []
        own = self.OWN.get(eng)
        if own is not None:
            waits = list(waits) + [(own, self.counts[own])]
        for w in waits:
            if w is None:
                continue
            s, v = w
            if v <= 0 or (eng == "tensor" and s == "sp") \
                    or v <= self.hwm[eng][s]:
                continue
            self.hwm[eng][s] = v
            ws.append((s, v))
        self.per_engine[eng].append((fn, inc, incval, ws))
        if inc is not None:
            self.counts[inc] += incval
            return (inc, self.counts[inc])
        return None

    def cnt(self, sem):
        return (sem, self.counts[sem])

    def dma(self, eng, sem, fn, waits=()):
        prev = (sem, self.counts[sem])
        return self.op(eng, fn, inc=sem, incval=16,
                       waits=list(waits) + [prev])

    def flush(self, nc, semmap):
        with nc.Block() as blk:
            for eng in ENGINES:
                ops = self.per_engine[eng]
                if not ops:
                    continue

                def make_body(ops=ops):
                    def body(e):
                        for fn, inc, incval, ws in ops:
                            for (s, v) in ws:
                                e.wait_ge(semmap[s], v)
                            ins = fn(e)
                            if inc is not None:
                                ins.then_inc(semmap[inc], incval)
                    return body

                getattr(blk, eng)(make_body())
        self.per_engine = {e: [] for e in ENGINES}


class Res:

    def __init__(self):
        self.w = None
        self.rs = []

    def wrote(self, tok):
        self.w = tok
        self.rs = []

    def read(self, tok):
        if tok is not None:
            self.rs.append(tok)

    def for_read(self):
        return [self.w] if self.w else []

    def for_write(self):
        out = list(self.rs)
        if self.w:
            out.append(self.w)
        return out


def build_program(phases=5):
    nc = bass.Bass("TRN2", target_bir_lowering=False)

    d_x = nc.dram_tensor("xb", [S, H], F32, kind="ExternalInput").ap()
    d_wdkv = nc.dram_tensor("w_dkv", [H, 512], F32, kind="ExternalInput").ap()
    d_wdq = nc.dram_tensor("w_dq", [H, 512], F32, kind="ExternalInput").ap()
    d_wuk = nc.dram_tensor("w_uk", [512, 1024], F32, kind="ExternalInput").ap()
    d_wuv = nc.dram_tensor("w_uv", [512, H], F32, kind="ExternalInput").ap()
    d_wuq = nc.dram_tensor("w_uq", [512, 1024], F32, kind="ExternalInput").ap()
    d_wkr = nc.dram_tensor("w_kr", [H, 1024], F32, kind="ExternalInput").ap()
    d_wqr = nc.dram_tensor("w_qr", [512, 1024], F32, kind="ExternalInput").ap()
    d_wout = nc.dram_tensor("w_out", [H, H], F32, kind="ExternalInput").ap()
    d_wg = nc.dram_tensor("w_gate", [H, INTER], I8, kind="ExternalInput").ap()
    d_wu = nc.dram_tensor("w_up", [H, INTER], I8, kind="ExternalInput").ap()
    d_wd = nc.dram_tensor("w_down", [INTER, H], I8, kind="ExternalInput").ap()
    d_sg = nc.dram_tensor("s_gate", [INTER], F32, kind="ExternalInput").ap()
    d_su = nc.dram_tensor("s_up", [INTER], F32, kind="ExternalInput").ap()
    d_sd = nc.dram_tensor("s_down", [H], F32, kind="ExternalInput").ap()
    d_ident = nc.dram_tensor("ident", [128, 128], F32, kind="ExternalInput").ap()
    d_posb = nc.dram_tensor("posb", [1, 2], F32, kind="ExternalInput").ap()
    d_y = nc.dram_tensor("y", [OWN, H], F32, kind="ExternalOutput").ap()
    d_mci = nc.dram_tensor("m_cc_in", [128, 2048], BF16).ap()
    d_mco = nc.dram_tensor("m_cc_out", [128, 2048], BF16).ap()

    sch = Sched()
    ctx = ExitStack()
    with ctx:
        sb = lambda n, s, d: ctx.enter_context(nc.sbuf_tensor(n, s, d))
        semmap = {s: ctx.enter_context(nc.semaphore(s)) for s in SEMS}

        identbf = sb("identbf", [128, 128], BF16)
        identf = sb("identf", [128, 128], F32)
        st_rstd = sb("st_rstd", [128, 32], F32)
        st_nmr = sb("st_nmr", [128, 32], F32)
        st_sum = sb("st_sum", [128, 32], F32)
        st_sq = sb("st_sq", [128, 32], F32)
        st_mean = sb("st_mean", [128, 32], F32)
        st_m2 = sb("st_m2", [128, 32], F32)
        st_var = sb("st_var", [128, 32], F32)
        sdown_rep = sb("sdown_rep", [128, H], F32)
        sg_col = sb("sg_col", [128, 64], F32)
        su_col = sb("su_col", [128, 64], F32)
        cols = sb("cols", [128, 8], F32)
        posb_sb = sb("posb_sb", [1, 2], F32)

        COL_EPS = cols[:, 0:1]
        COL_PI = cols[:, 1:2]
        COL_HPI = cols[:, 2:3]
        COL_NEG1 = cols[:, 3:4]
        COL_LN40 = cols[:, 4:5]
        COL_LNIF = cols[:, 5:6]

        r_tab = Res()
        r_stats = Res()

        abc1 = ExitStack()
        cosA = abc1.enter_context(nc.sbuf_tensor("cosA", [128, 512], F32))
        sinA = abc1.enter_context(nc.sbuf_tensor("sinA", [128, 512], F32))
        ckv_fm = abc1.enter_context(
            nc.sbuf_tensor("ckv_fm", [128, 4 * OWN], BF16))
        Msb = abc1.enter_context(nc.sbuf_tensor("Msb", [128, 2048], F32))

        if phases <= 1:
            return nc
        with ExitStack() as actx:
            asb = lambda n, s, d: actx.enter_context(nc.sbuf_tensor(n, s, d))
            psA = actx.enter_context(nc.psum_tensor("psA", [128, 1024], F32))
            psTrA = actx.enter_context(nc.psum_tensor("psTrA", [128, 2048], BF16))
            psM = actx.enter_context(nc.psum_tensor("psM", [128, 2048], F32))

            wdkv = asb("wdkv", [128, 16 * 512], BF16)
            wdq = asb("wdq", [128, 16 * 512], BF16)
            wuv = asb("wuv", [128, 4 * 2048], BF16)
            wuqr = asb("wuqr", [128, 4 * 2048], BF16)
            xch = [asb(f"xch{i}", [128, 2048], F32) for i in range(4)]
            dump = asb("dump", [128, 2048], BF16)
            xn_fm = asb("xn_fm", [128, 16 * 512], BF16)
            cq_fm = asb("cq_fm", [128, 4 * 512], BF16)
            ckv_sc = None
            v_tm = [asb(f"v_tm{i}", [128, 2048], BF16) for i in range(2)]
            q_tm = [asb(f"q_tm{i}", [128, 2048], BF16) for i in range(2)]
            ropeT = asb("ropeT", [128, 1024], F32)

            r_x = [Res() for _ in range(4)]
            r_xnfm = Res()
            r_cq = Res()
            r_ckvsc = Res()
            r_v = [Res() for _ in range(2)]
            r_q = [Res() for _ in range(2)]
            r_psA = [Res() for _ in range(2)]
            r_psTrA = [Res(), Res()]
            r_psM = Res()
            r_M = Res()
            psA_slot = [psA[:, i * 512:(i + 1) * 512] for i in range(2)]
            rot = [0]

            def next_psA():
                i = rot[0] % 2
                rot[0] += 1
                return i, psA_slot[i]

            qr_st = [asb(f"qr_st{i}", [128, 1024], F32) for i in range(2)]
            xn_tm4 = [asb(f"xn_tm4_{i}", [128, 2048], BF16) for i in range(4)]
            jrow = asb("jrow", [1, 16], F32)
            invf = asb("invf", [1, 16], F32)
            r_qst = [Res(), Res()]
            r_xntm4 = [Res() for _ in range(4)]
            head_tn = {}
            norm_tok = {}

            angle = ropeT[:, 0:256]
            acc = ropeT[:, 256:512]
            mred = ropeT[:, 512:768]
            t_posb = sch.dma("sync", "dI", lambda e: e.dma_start(
                posb_sb[:, :], d_posb[:, :]))
            t_idb = sch.dma("gpsimd", "dIb", lambda e: e.dma_start(
                identbf[:, :], d_ident[:, :]))
            t_cols = None
            for i, val in [(0, 1e-5), (1, PI), (2, HALF_PI), (3, -1.0),
                           (4, -math.log(40.0)), (5, -math.log(10000.0) / 16.0)]:
                t_cols = sch.op("gpsimd",
                                lambda e, i=i, v=val: e.memset(cols[:, i:i + 1], v),
                                inc="sg")
            t_j = sch.op("gpsimd", lambda e: e.iota(
                jrow[:, :], [[1, 16]], base=0, channel_multiplier=0,
                allow_small_or_imprecise_dtypes=True), inc="sg")
            t_invf = sch.op("scalar", lambda e: e.activation(
                invf[:, :], jrow[:, :], AF.Exp,
                bias=COL_LN40[0:1, :], scale=COL_LNIF[0:1, :]),
                inc="sa", waits=[t_j, t_cols])
            t_add = [None, None]
            for i in range(2):
                t_io = sch.op("gpsimd", lambda e, i=i: e.iota(
                    qr_st[i][0:1, 0:1024], [[1, 1024]], base=i * 1024,
                    channel_multiplier=0,
                    allow_small_or_imprecise_dtypes=True), inc="sg")
                t_add[i] = sch.op("vector", lambda e, i=i: e.tensor_scalar_add(
                    qr_st[i][0:1, 0:1024], qr_st[i][0:1, 0:1024],
                    posb_sb[0:1, 0:1]), inc="sv", waits=[t_io, t_posb])
                r_qst[i].wrote(t_add[i])
            t_mm = None
            for bc in range(16):
                qi, off = bc // 8, (bc % 8) * 128
                t_mm = sch.op("tensor", lambda e, qi=qi, off=off, bc=bc:
                              e.matmul(psA[:, bc * 16:(bc + 1) * 16],
                                       qr_st[qi][0:1, off:off + 128],
                                       invf[:, :], start=True, stop=True),
                              inc="sp" if bc == 15 else None,
                              waits=[t_add[0], t_add[1], t_invf]
                              if bc == 0 else [])
            r_psA[0].wrote(t_mm)
            r_qst[0].read(t_mm)
            r_qst[1].read(t_mm)
            t_angle = sch.op("vector", lambda e: e.tensor_copy(
                angle, psA[:, 0:256]), inc="sv", waits=[t_mm])
            r_psA[0].read(t_angle)

            def cascade(shift, out_tile, bias_col, prev_act):
                t = sch.op("vector", lambda e: e.tensor_single_scalar(
                    acc, angle, TWO_PI - shift, ALU.is_ge),
                    inc="sv", waits=[t_angle, prev_act])
                for k in range(2, NRED + 1):
                    t = sch.op("vector", lambda e, k=k: e.scalar_tensor_tensor(
                        acc, angle, TWO_PI * k - shift, acc,
                        ALU.is_ge, ALU.add), inc="sv")
                t = sch.op("vector", lambda e: e.scalar_tensor_tensor(
                    mred, acc, -TWO_PI, angle, ALU.mult,
                    ALU.add), inc="sv")
                return sch.op("scalar", lambda e, o=out_tile, b=bias_col:
                              e.activation(o, mred, AF.Sin, bias=b[:, :],
                                           scale=COL_NEG1[:, :]),
                              inc="sa", waits=[t, t_cols])

            t_sin = cascade(0.0, sinA[:, 0:256], COL_PI, None)
            t_cos = cascade(HALF_PI, cosA[:, 0:256], COL_HPI, t_sin)
            r_tab.wrote(t_cos)

            t_wdkv = sch.dma("gpsimd", "dwdkv", lambda e: e.dma_start(
                wdkv.rearrange("p (kc c) -> p kc c", c=512),
                d_wdkv.rearrange("(kc p) c -> p kc c", p=128)))
            t_wdq = sch.dma("gpsimd", "dwdq", lambda e: e.dma_start(
                wdq.rearrange("p (kc c) -> p kc c", c=512),
                d_wdq.rearrange("(kc p) c -> p kc c", p=128)))
            t_wuv = sch.dma("gpsimd", "dwuv", lambda e: e.dma_start(
                wuv.rearrange("p (cc f) -> p cc f", f=2048),
                d_wuv.rearrange("(cc p) f -> p cc f", p=128)))
            t_wuq = sch.dma("gpsimd", "dwuqr", lambda e: e.dma_start(
                wuqr.rearrange("p (cc two f) -> p cc two f", two=2, f=1024)
                [:, :, 0, :],
                d_wuq.rearrange("(cc p) f -> p cc f", p=128)))
            t_wqr = sch.dma("gpsimd", "dk0", lambda e: e.dma_start(
                wuqr.rearrange("p (cc two f) -> p cc two f", two=2, f=1024)
                [:, :, 1, :],
                d_wqr.rearrange("(cc p) f -> p cc f", p=128)))

            def head_A(tile):
                t0 = tile * TA
                bc0 = tile * 4
                sum_toks = []
                for kc in range(4):
                    xi = kc
                    bc = bc0 + kc
                    row = t0 + kc * 128
                    t_xa = sch.dma("sync", f"dx{xi}",
                                   lambda e, r=row, i=xi: e.dma_start(
                                       xch[i][:, 0:1024],
                                       d_x[r:r + 128, 0:1024]),
                                   waits=r_x[xi].for_write())
                    t_xb = sch.dma("sync", XS2[xi],
                                   lambda e, r=row, i=xi: e.dma_start(
                                       xch[i][:, 1024:2048],
                                       d_x[r:r + 128, 1024:2048]),
                                   waits=r_x[xi].for_write())
                    r_x[xi].wrote(t_xb)
                    t_s = sch.op("vector", lambda e, i=xi, bc=bc: e.reduce_sum(
                        st_sum[:, bc:bc + 1], xch[i][:, :], AXX),
                        inc="sv", waits=[t_xa, t_xb])
                    t_q2 = sch.op("scalar", lambda e, i=xi, bc=bc: e.activation(
                        dump[:, :], xch[i][:, :], AF.Square,
                        accum_out=st_sq[:, bc:bc + 1]), inc="sa",
                        waits=[t_xa, t_xb])
                    r_x[xi].read(t_s)
                    r_x[xi].read(t_q2)
                    sum_toks += [t_s, t_q2]
                sl = slice(bc0, bc0 + 4)
                sch.op("vector", lambda e, sl=sl: e.tensor_scalar_mul(
                    st_mean[:, sl], st_sum[:, sl], 1.0 / H), inc="sv",
                    waits=sum_toks)
                sch.op("vector", lambda e, sl=sl: e.tensor_scalar_mul(
                    st_var[:, sl], st_sq[:, sl], 1.0 / H), inc="sv")
                sch.op("vector", lambda e, sl=sl: e.tensor_mul(
                    st_m2[:, sl], st_mean[:, sl], st_mean[:, sl]), inc="sv")
                t4 = sch.op("vector", lambda e, sl=sl: e.tensor_sub(
                    st_var[:, sl], st_var[:, sl], st_m2[:, sl]), inc="sv")
                t5 = sch.op("scalar", lambda e, sl=sl: e.activation(
                    st_var[:, sl], st_var[:, sl], AF.Sqrt, bias=COL_EPS[:, :]),
                    inc="sa", waits=[t4, t_cols])
                t6 = sch.op("vector", lambda e, sl=sl: e.reciprocal(
                    st_rstd[:, sl], st_var[:, sl]), inc="sv", waits=[t5])
                t7 = sch.op("vector", lambda e, sl=sl: e.scalar_tensor_tensor(
                    st_nmr[:, sl], st_mean[:, sl], -1.0, st_rstd[:, sl],
                    ALU.mult, ALU.mult), inc="sv")
                if tile < 4:
                    r_stats.wrote(t7)
                head_tn[tile] = (t6, t7)

            def norm_A(tile):
                t6, t7 = head_tn[tile]
                bc0 = tile * 4
                tns = []
                for kc in range(4):
                    xi = kc
                    bc = bc0 + kc
                    t_n = sch.op("gpsimd", lambda e, i=xi, kc=kc, bc=bc:
                                 e.tensor_scalar(
                                     xn_tm4[kc][:, :], xch[i][:, :],
                                     st_rstd[:, bc:bc + 1],
                                     st_nmr[:, bc:bc + 1],
                                     ALU.mult, ALU.add),
                                 inc="sg",
                                 waits=[t7, t6] + r_xntm4[kc].for_write())
                    r_x[xi].read(t_n)
                    r_xntm4[kc].wrote(t_n)
                    tns.append(t_n)
                norm_tok[tile] = tns

            g_toks = {}

            def tr_A(tile):
                g_tok = {}
                for g in range(2):
                    for kc in range(4):
                        ni = kc
                        t_n = norm_tok[tile][kc]
                        tp = (g * 4 + kc) % 2
                        tt = None
                        for f8 in range(8):
                            fc = g * 8 + f8
                            tt = sch.op("tensor",
                                        lambda e, ni=ni, fc=fc, f8=f8, tp=tp:
                                        e.transpose(
                                            psTrA[:, tp * 1024 + f8 * 128:
                                                  tp * 1024 + (f8 + 1) * 128],
                                            xn_tm4[ni][:, fc * 128:(fc + 1) * 128],
                                            identbf[:, :]),
                                        inc="sp" if f8 == 7 else None,
                                        waits=([t_n, t_idb]
                                               + r_psTrA[tp].for_write())
                                        if f8 == 0 else [])
                        r_psTrA[tp].wrote(tt)
                        r_xntm4[ni].read(tt)
                        dst = xn_fm.rearrange("p (fc t) -> p fc t", t=512)[
                            :, g * 8:(g + 1) * 8, kc * 128:(kc + 1) * 128]
                        src_ = psTrA[:, tp * 1024:(tp + 1) * 1024] \
                            .rearrange("p (b t) -> p b t", t=128)
                        if tp == 0:
                            t_cp = sch.op("scalar", lambda e, d=dst, s=src_:
                                          e.activation(d, s, AF.Copy),
                                          inc="sa",
                                          waits=[tt] + r_xnfm.for_write())
                        else:
                            t_cp = sch.op("vector", lambda e, d=dst, s=src_:
                                          e.tensor_copy(d, s), inc="sv",
                                          waits=[tt] + r_xnfm.for_write())
                        r_psTrA[tp].read(t_cp)
                    g_tok[g] = [sch.cnt("sa"), sch.cnt("sv")]
                r_xnfm.wrote(sch.cnt("sv"))
                r_xnfm.read(sch.cnt("sa"))
                g_toks[tile] = g_tok

            def body_A(tile):
                own = tile < 4
                t0 = tile * TA
                bc0 = tile * 4
                g_tok = g_toks[tile]

                for which in range(2):
                    wt = wdkv if which == 0 else wdq
                    t_w = t_wdkv if which == 0 else t_wdq
                    for cc in range(4):
                        si, slot = next_psA()
                        tt = None
                        for kk in range(16):
                            w0 = []
                            if kk == 0:
                                w0 = g_tok[0] + [t_w] + r_psA[si].for_write()
                            elif kk == 8:
                                w0 = list(g_tok[1])
                            tt = sch.op("tensor",
                                        lambda e, w=wt, kk=kk, cc=cc, sl=slot:
                                        e.matmul(sl,
                                                 w[:, kk * 512 + cc * 128:
                                                   kk * 512 + (cc + 1) * 128],
                                                 xn_fm[:, kk * 512:(kk + 1) * 512],
                                                 start=(kk == 0), stop=(kk == 15)),
                                        inc="sp" if kk == 15 else None,
                                        waits=w0)
                        r_psA[si].wrote(tt)
                        if which == 0:
                            if own:
                                dst = ckv_fm[:, cc * OWN + t0:cc * OWN + t0 + 512]
                                rr = None
                            else:
                                dst = ckv_sc[:, cc * 512:(cc + 1) * 512]
                                rr = r_ckvsc
                        else:
                            dst = cq_fm[:, cc * 512:(cc + 1) * 512]
                            rr = r_cq
                        t_e = sch.op("vector", lambda e, d=dst, sl=slot:
                                     e.tensor_copy(d, sl), inc="sv",
                                     waits=[tt] + (rr.for_write() if rr else []))
                        r_psA[si].read(t_e)
                        if rr:
                            rr.wrote(t_e)
                ckv_ready = sch.cnt("sv")
                r_xnfm.read(sch.cnt("sp"))
                if tile + 1 < NT_A_RUN:
                    tr_A(tile + 1)

                vq_tok = {}

                def emit_M(tcl):
                    vb = tcl % 2
                    v_ready, q_ready = vq_tok[tcl]
                    tt = None
                    for h in range(NH):
                        first = (tcl == 0 and (h % 4) == 0)
                        lastb = (tcl == 3 and (h % 4) == 3)
                        w0 = []
                        if h == 0:
                            w0 = q_ready + list(v_ready)
                            if tcl == 0:
                                w0 = w0 + r_psM.for_write()
                        tt = sch.op("tensor", lambda e, h=h, f=first, l=lastb,
                                    i=vb: e.matmul(
                                        psM[:, h * 128:(h + 1) * 128],
                                        v_tm[i][:, h * 128:(h + 1) * 128],
                                        q_tm[i][:, h * 128:(h + 1) * 128],
                                        start=f, stop=l),
                                    inc="sp" if h == 15 else None, waits=w0)
                    r_v[vb].read(tt)
                    r_q[vb].read(tt)
                    if tcl == 3:
                        r_psM.wrote(tt)
                        if tile == 0:
                            t_m = sch.op("vector", lambda e: e.tensor_copy(
                                Msb[:, :], psM[:, :]), inc="sv", waits=[tt])
                        else:
                            t_m = sch.op("vector", lambda e: e.tensor_add(
                                Msb[:, :], Msb[:, :], psM[:, :]), inc="sv",
                                waits=[tt])
                        r_psM.read(t_m)
                        r_M.wrote(t_m)

                for tcl in range(4):
                    bc = bc0 + tcl
                    vb = tcl % 2
                    for seg in range(4):
                        si, slot = next_psA()
                        tt = None
                        for cc in range(4):
                            if own:
                                l_ap = ckv_fm[:, cc * OWN + t0 + tcl * 128:
                                              cc * OWN + t0 + (tcl + 1) * 128]
                            else:
                                l_ap = ckv_sc[:, cc * 512 + tcl * 128:
                                              cc * 512 + (tcl + 1) * 128]
                            tt = sch.op("tensor",
                                        lambda e, l=l_ap, cc=cc, s=seg, sl=slot:
                                        e.matmul(sl, l,
                                                 wuv[:, cc * 2048 + s * 512:
                                                     cc * 2048 + (s + 1) * 512],
                                                 start=(cc == 0), stop=(cc == 3)),
                                        inc="sp" if cc == 3 else None,
                                        waits=([ckv_ready, t_wuv]
                                               + r_psA[si].for_write())
                                        if cc == 0 else [])
                        r_psA[si].wrote(tt)
                        t_e = sch.op("scalar", lambda e, s=seg, i=vb, sl=slot:
                                     e.activation(
                                         v_tm[i][:, s * 512:(s + 1) * 512], sl,
                                         AF.Copy), inc="sa",
                                     waits=[tt] + r_v[vb].for_write())
                        r_psA[si].read(t_e)
                    r_v[vb].wrote(sch.cnt("sa"))
                    v_ready = [sch.cnt("sa")]

                    for seg in range(4):
                        si, slot = next_psA()
                        tt = None
                        for cc in range(4):
                            tt = sch.op("tensor",
                                        lambda e, cc=cc, s=seg, sl=slot: e.matmul(
                                            sl,
                                            cq_fm[:, cc * 512 + tcl * 128:
                                                  cc * 512 + (tcl + 1) * 128],
                                            wuqr[:, cc * 2048 + s * 512:
                                                 cc * 2048 + (s + 1) * 512],
                                            start=(cc == 0), stop=(cc == 3)),
                                        inc="sp" if cc == 3 else None,
                                        waits=([ckv_ready, t_wuq, t_wqr]
                                               + r_psA[si].for_write())
                                        if cc == 0 else [])
                        r_psA[si].wrote(tt)
                        if seg < 2:
                            src = slot.rearrange("p (h d) -> p h d", d=64)
                            dst = q_tm[vb].rearrange("p (h d) -> p h d", d=128)[
                                :, seg * 8:(seg + 1) * 8, 0:64]
                            t_e = sch.op("scalar", lambda e, s=src, dd=dst:
                                         e.activation(dd, s, AF.Copy), inc="sa",
                                         waits=[tt] + r_q[vb].for_write())
                            r_psA[si].read(t_e)
                            r_q[vb].wrote(t_e)
                        else:
                            soff = (seg - 2) * 512
                            t_qe = sch.op("scalar", lambda e, sl=slot, i=vb,
                                          so=soff: e.activation(
                                              qr_st[i][:, so:so + 512], sl,
                                              AF.Copy), inc="sa",
                                          waits=[tt] + r_qst[vb].for_write())
                            r_psA[si].read(t_qe)
                            r_qst[vb].wrote(t_qe)
                            h0 = (seg - 2) * 8
                            qr = qr_st[vb][:, soff:soff + 512] \
                                .rearrange("p (h d) -> p h d", d=64)
                            qq = q_tm[vb].rearrange("p (h d) -> p h d", d=128)
                            c16 = cosA[:, bc * 16:bc * 16 + 16].unsqueeze(1) \
                                .broadcast_to([128, 8, 16])
                            s16 = sinA[:, bc * 16:bc * 16 + 16].unsqueeze(1) \
                                .broadcast_to([128, 8, 16])
                            rT = [ropeT.rearrange("p (s t) -> p s t", t=256)
                                  [:, i, :].rearrange("p (h j) -> p h j", j=16)
                                  [:, 0:8, :] for i in range(4)]
                            w0 = [t_qe, r_tab.w] + r_q[vb].for_write()
                            sch.op("vector", lambda e, a=rT[0], q=qr, c=c16:
                                   e.tensor_tensor(a, q[:, :, 0:16], c, ALU.mult),
                                   inc="sv", waits=w0)
                            sch.op("vector", lambda e, a=rT[1], q=qr, s=s16:
                                   e.tensor_tensor(a, q[:, :, 16:32], s, ALU.mult),
                                   inc="sv")
                            sch.op("vector", lambda e, a=rT[0], b2=rT[1], d=qq,
                                   h0=h0: e.tensor_sub(d[:, h0:h0 + 8, 64:80],
                                                       a, b2), inc="sv")
                            sch.op("vector", lambda e, a=rT[2], q=qr, c=c16:
                                   e.tensor_tensor(a, q[:, :, 16:32], c, ALU.mult),
                                   inc="sv")
                            sch.op("vector", lambda e, a=rT[3], q=qr, s=s16:
                                   e.tensor_tensor(a, q[:, :, 0:16], s, ALU.mult),
                                   inc="sv")
                            sch.op("vector", lambda e, a=rT[2], b2=rT[3], d=qq,
                                   h0=h0: e.tensor_add(d[:, h0:h0 + 8, 80:96],
                                                       a, b2), inc="sv")
                            tg = sch.op("vector", lambda e, q=qr, d=qq, h0=h0:
                                        e.tensor_copy(d[:, h0:h0 + 8, 96:128],
                                                      q[:, :, 32:64]), inc="sv")
                            r_qst[vb].read(tg)
                            r_q[vb].wrote(tg)
                    vq_tok[tcl] = (v_ready, [sch.cnt("sa"), sch.cnt("sv")])
                    if tcl >= 1:
                        emit_M(tcl - 1)
                emit_M(3)
                pe_now = sch.cnt("sp")
                r_cq.read(pe_now)
                r_ckvsc.read(pe_now)

            head_A(0)
            norm_A(0)
            head_A(1)
            tr_A(0)
            for tile in range(NT_A_RUN):
                if tile + 1 < NT_A_RUN:
                    norm_A(tile + 1)
                body_A(tile)
                if tile + 2 < NT_A_RUN:
                    head_A(tile + 2)

            msb_bf = Msb.bitcast(BF16)[:, 0:2048]
            t_sc = sch.op("vector", lambda e: e.tensor_scalar_mul(
                msb_bf, Msb[:, :], ATTN_SCALE), inc="sv",
                waits=r_M.for_read())
            r_M.read(t_sc)
            t_up = sch.dma("gpsimd", "dmc", lambda e: e.dma_start(
                d_mci[:, :], msb_bf), waits=[t_sc])
            t_cc = sch.op("gpsimd", lambda e: e.collective_compute(
                "AllReduce", ALU.add,
                replica_groups=[[0, 1], [2, 3], [4, 5], [6, 7]],
                ins=[d_mci.opt()], outs=[d_mco.opt()]),
                inc="scc", incval=1, waits=[t_up])
            mex_tok = sch.dma("gpsimd", "dmc", lambda e: e.dma_start(
                msb_bf, d_mco[:, :]), waits=[t_cc])
            r_M.wrote(mex_tok)

            sch.flush(nc, semmap)

        if phases <= 2:
            return nc
        with ExitStack() as wctx:
            wsb = lambda n, s, d: wctx.enter_context(nc.sbuf_tensor(n, s, d))
            w2_sb = wsb("w2_sb", [128, 4 * 2048], BF16)
            wrope_sb = wsb("wrope_sb", [128, 8 * 2048], BF16)

            with ExitStack() as bctx:
                bsb = lambda n, s, d: bctx.enter_context(nc.sbuf_tensor(n, s, d))
                psB = bctx.enter_context(nc.psum_tensor("psB", [128, 2048], F32))
                psTrB = bctx.enter_context(
                    nc.psum_tensor("psTrB", [128, 1024], BF16))
                Mt_bf = bsb("Mt_bf", [128, 2048], BF16)
                wout_b = [bsb(f"wout_b{i}", [128, 2048], BF16) for i in range(4)]
                wuk_bf = bsb("wuk_bf", [128, 4 * 1024], BF16)
                wukT = bsb("wukT", [128, 8 * 512], BF16)
                wc_st = bsb("wc_st", [128, 8 * 2048], BF16)

                t_mt = sch.op("vector", lambda e: e.tensor_copy(
                    Mt_bf[:, :], msb_bf), inc="sv", waits=[mex_tok])
                t_wuk = sch.dma("gpsimd", "dwuk", lambda e: e.dma_start(
                    wuk_bf.rearrange("p (cc f) -> p cc f", f=1024),
                    d_wuk.rearrange("(cc p) f -> p cc f", p=128)))

                r_psTrB = Res()
                for ph in range(8 if BSUB >= 2 else 0):
                    tt = None
                    for hp in range(2):
                        h = 2 * ph + hp
                        for cc2 in range(4):
                            slot = psTrB[hp * 64:hp * 64 + 64,
                                         cc2 * 128:(cc2 + 1) * 128]
                            tt = sch.op("tensor",
                                        lambda e, h=h, c=cc2, sl=slot:
                                        e.transpose(
                                            sl,
                                            wuk_bf[:, c * 1024 + h * 64:
                                                   c * 1024 + (h + 1) * 64],
                                            identbf[:, :]),
                                        inc="sp" if (hp == 1 and cc2 == 3)
                                        else None,
                                        waits=([t_wuk] + r_psTrB.for_write())
                                        if (hp == 0 and cc2 == 0) else [])
                    r_psTrB.wrote(tt)
                    t_cp = sch.op("vector", lambda e, ph=ph: e.tensor_copy(
                        wukT[:, ph * 512:(ph + 1) * 512], psTrB[:, 0:512]),
                        inc="sv", waits=[tt])
                    r_psTrB.read(t_cp)
                wukT_ready = sch.cnt("sv")

                r_psBs = [Res() for _ in range(4)]
                r_woutb = [Res() for _ in range(4)]
                for h in range(NH if BSUB >= 3 else 0):
                    wi = h % 4
                    t_w = sch.dma("gpsimd", f"dwo{wi}",
                                  lambda e, h=h, i=wi: e.dma_start(
                                      wout_b[i][:, :],
                                      d_wout[h * 128:(h + 1) * 128, :]),
                                  waits=r_woutb[wi].for_write())
                    r_woutb[wi].wrote(t_w)
                    for seg in range(4):
                        tt = sch.op("tensor", lambda e, h=h, s=seg, i=wi:
                                    e.matmul(psB[:, s * 512:(s + 1) * 512],
                                             Mt_bf[:, h * 128:(h + 1) * 128],
                                             wout_b[i][:, s * 512:(s + 1) * 512],
                                             start=True, stop=True),
                                    inc="sp",
                                    waits=([t_w, t_mt] if seg == 0 else [])
                                    + r_psBs[seg].for_write())
                        r_psBs[seg].wrote(tt)
                        d1 = wc_st[(h % 2) * 64:(h % 2) * 64 + 64,
                                   (h // 2) * 2048 + seg * 512:
                                   (h // 2) * 2048 + (seg + 1) * 512]
                        d2 = wrope_sb[(h % 2) * 64:(h % 2) * 64 + 64,
                                      (h // 2) * 2048 + seg * 512:
                                      (h // 2) * 2048 + (seg + 1) * 512]
                        t_e1 = sch.op("scalar", lambda e, d=d1, s=seg:
                                      e.activation(
                                          d, psB[0:64, s * 512:(s + 1) * 512],
                                          AF.Copy), inc="sa", waits=[tt])
                        t_e2 = sch.op("vector", lambda e, d=d2, s=seg:
                                      e.tensor_copy(
                                          d, psB[64:128,
                                                 s * 512:(s + 1) * 512]),
                                      inc="sv", waits=[tt])
                        r_psBs[seg].read(t_e1)
                        r_psBs[seg].read(t_e2)
                    r_woutb[wi].read(sch.cnt("sp"))
                wc_ready = [sch.cnt("sv"), sch.cnt("sa")]

                for cc in range(4 if BSUB >= 4 else 0):
                    tt = None
                    for seg in range(4):
                        for ph in range(8):
                            l_ap = wukT[:, ph * 512 + cc * 128:
                                        ph * 512 + (cc + 1) * 128]
                            rh = wc_st[:, ph * 2048 + seg * 512:
                                       ph * 2048 + (seg + 1) * 512]
                            tt = sch.op("tensor",
                                        lambda e, l=l_ap, r=rh, s=seg, ph=ph:
                                        e.matmul(psB[:, s * 512:(s + 1) * 512],
                                                 l, r, start=(ph == 0),
                                                 stop=(ph == 7)),
                                        inc="sp" if (ph == 7 and seg == 3)
                                        else None,
                                        waits=(list(wc_ready) + [wukT_ready]
                                               + r_psBs[0].for_write()
                                               + r_psBs[1].for_write()
                                               + r_psBs[2].for_write()
                                               + r_psBs[3].for_write())
                                        if (ph == 0 and seg == 0) else [])
                    for s4 in range(4):
                        r_psBs[s4].wrote(tt)
                    t_e = sch.op("vector", lambda e, cc=cc: e.tensor_copy(
                        w2_sb[:, cc * 2048:(cc + 1) * 2048], psB[:, :]),
                        inc="sv", waits=[tt])
                    for s4 in range(4):
                        r_psBs[s4].read(t_e)

                sch.flush(nc, semmap)

            if phases <= 3:
                return nc
            with ExitStack() as c1ctx:
                csb = lambda n, s, d: c1ctx.enter_context(nc.sbuf_tensor(n, s, d))
                psC = c1ctx.enter_context(nc.psum_tensor("psC", [128, 2048], F32))
                psTr1 = c1ctx.enter_context(
                    nc.psum_tensor("psTr1", [128, 2048], BF16))
                wkr_bf = csb("wkr_bf", [128, 16 * 1024], BF16)
                t_wkr = sch.dma("gpsimd", "dwkr", lambda e: e.dma_start(
                    wkr_bf.rearrange("p (kc f) -> p kc f", f=1024),
                    d_wkr.rearrange("(kc p) f -> p kc f", p=128)))
                xn_fm1 = csb("xn_fm1", [128, 16 * 512], BF16)
                xc1 = [csb(f"xc1_{i}", [128, 2048], F32) for i in range(2)]
                krope_tm = [csb(f"krope_tm{i}", [128, 1024], BF16)
                            for i in range(4)]
                krope_fm = csb("krope_fm", [128, 8 * 512], BF16)
                outst = [csb("outst0", [128, 2048], F32)]
                ropeT1 = csb("ropeT1", [128, 1024], F32)


                r_x1 = [Res() for _ in range(2)]
                r_xnfm1 = Res()
                r_kr = [Res() for _ in range(4)]
                r_krfm = Res()
                r_out = [Res()]
                r_psC = Res()
                r_psC4 = [Res() for _ in range(4)]
                r_psTr1 = [Res(), Res()]

                kr_st = [csb(f"kr_st{i}", [128, 1024], F32) for i in range(2)]
                xn_tm1_4 = [csb(f"xn_tm1_4_{i}", [128, 2048], BF16)
                            for i in range(4)]
                r_krst = [Res(), Res()]
                r_xntm1_4 = [Res() for _ in range(4)]
                head_tn1 = {}
                norm_tok1 = {}

                def head_C1(tile):
                    t0 = tile * TC
                    tns = []
                    for kc in range(4):
                        xi = kc % 2
                        row = t0 + kc * 128
                        idx = tile * 4 + kc
                        t_xa = sch.dma("sync", f"dx{xi}",
                                       lambda e, r=row, i=xi: e.dma_start(
                                           xc1[i][:, 0:1024],
                                           d_x[r:r + 128, 0:1024]),
                                       waits=r_x1[xi].for_write())
                        t_xb = sch.dma("sync", f"dx{xi + 2}",
                                       lambda e, r=row, i=xi: e.dma_start(
                                           xc1[i][:, 1024:2048],
                                           d_x[r:r + 128, 1024:2048]),
                                       waits=r_x1[xi].for_write())
                        r_x1[xi].wrote(t_xb)
                        t_n = sch.op("gpsimd", lambda e, i=xi, kc=kc, idx=idx:
                                     e.tensor_scalar(
                                         xn_tm1_4[kc][:, :], xc1[i][:, :],
                                         st_rstd[:, idx:idx + 1],
                                         st_nmr[:, idx:idx + 1],
                                         ALU.mult, ALU.add),
                                     inc="sg",
                                     waits=[t_xa, t_xb]
                                     + r_xntm1_4[kc].for_write())
                        r_x1[xi].read(t_n)
                        r_xntm1_4[kc].wrote(t_n)
                        tns.append(t_n)
                    norm_tok1[tile] = tns

                xn_toks1 = {}

                def tr_C1(tile):
                    for kc in range(4):
                        ni = kc
                        t_n = norm_tok1[tile][kc]
                        for g in range(2):
                            tp = (kc * 2 + g) % 2
                            tt = None
                            for f8 in range(8):
                                fc = g * 8 + f8
                                tt = sch.op(
                                    "tensor",
                                    lambda e, ni=ni, fc=fc, f8=f8, tp=tp:
                                    e.transpose(
                                        psTr1[:, tp * 1024 + f8 * 128:
                                              tp * 1024 + (f8 + 1) * 128],
                                        xn_tm1_4[ni][:, fc * 128:(fc + 1) * 128],
                                        identbf[:, :]),
                                    inc="sp" if f8 == 7 else None,
                                    waits=([t_n] + r_psTr1[tp].for_write())
                                    if f8 == 0 else [])
                            r_psTr1[tp].wrote(tt)
                            r_xntm1_4[ni].read(tt)
                            dst = xn_fm1.rearrange("p (fc t) -> p fc t", t=512)[
                                :, g * 8:(g + 1) * 8, kc * 128:(kc + 1) * 128]
                            src_ = psTr1[:, tp * 1024:(tp + 1) * 1024] \
                                .rearrange("p (b t) -> p b t", t=128)
                            t_cp = sch.op("scalar", lambda e, d=dst, s=src_:
                                          e.activation(d, s, AF.Copy), inc="sa",
                                          waits=[tt] + r_xnfm1.for_write())
                            r_psTr1[tp].read(t_cp)
                    r_xnfm1.wrote(sch.cnt("sa"))
                    xn_toks1[tile] = sch.cnt("sa")

                def body_C1(tile):
                    t0 = tile * TC
                    xn_ready = xn_toks1[tile]

                    kr_done = {}
                    for half in range(2):
                        tt = None
                        for kk in range(16):
                            for tc2 in range(2):
                                tcl = half * 2 + tc2
                                for seg in range(2):
                                    w0 = []
                                    if kk == 0:
                                        w0 = r_psC4[tc2 * 2 + seg].for_write()
                                        if tc2 == 0 and seg == 0:
                                            w0 = w0 + [xn_ready, t_wkr]
                                    tt = sch.op(
                                        "tensor",
                                        lambda e, kk=kk, t=tcl, tc2=tc2, s=seg:
                                        e.matmul(
                                            psC[:, tc2 * 1024 + s * 512:
                                                tc2 * 1024 + (s + 1) * 512],
                                            xn_fm1[:, kk * 512 + t * 128:
                                                   kk * 512 + (t + 1) * 128],
                                            wkr_bf[:, kk * 1024 + s * 512:
                                                   kk * 1024 + (s + 1) * 512],
                                            start=(kk == 0), stop=(kk == 15)),
                                        inc="sp" if (kk == 15 and tc2 == 1
                                                     and seg == 1) else None,
                                        waits=w0)
                        for tc2 in range(2):
                            for seg in range(2):
                                r_psC4[tc2 * 2 + seg].wrote(tt)
                        for tc2 in range(2):
                            tcl = half * 2 + tc2
                            bc = tile * 4 + tcl
                            ki = tcl
                            si2 = tcl % 2
                            t_ke = sch.op("scalar", lambda e, tc2=tc2, i=si2:
                                          e.activation(
                                              kr_st[i][:, :],
                                              psC[:, tc2 * 1024:
                                                  (tc2 + 1) * 1024],
                                              AF.Copy), inc="sa",
                                          waits=[tt] + r_krst[si2].for_write())
                            r_psC4[tc2 * 2].read(t_ke)
                            r_psC4[tc2 * 2 + 1].read(t_ke)
                            r_krst[si2].wrote(t_ke)
                            kr = kr_st[si2].rearrange("p (h d) -> p h d", d=64)
                            kd = krope_tm[ki].rearrange("p (h d) -> p h d", d=64)
                            c16 = cosA[:, bc * 16:bc * 16 + 16].unsqueeze(1) \
                                .broadcast_to([128, 16, 16])
                            s16 = sinA[:, bc * 16:bc * 16 + 16].unsqueeze(1) \
                                .broadcast_to([128, 16, 16])
                            rT = [ropeT1.rearrange("p (s t) -> p s t", t=256)
                                  [:, i, :].rearrange("p (h j) -> p h j", j=16)
                                  for i in range(4)]
                            w0 = [t_ke] + r_kr[ki].for_write()
                            sch.op("vector", lambda e, a=rT[0], k=kr, c=c16:
                                   e.tensor_tensor(a, k[:, :, 0:16], c, ALU.mult),
                                   inc="sv", waits=w0)
                            sch.op("vector", lambda e, a=rT[1], k=kr, s=s16:
                                   e.tensor_tensor(a, k[:, :, 16:32], s,
                                                   ALU.mult), inc="sv")
                            sch.op("vector", lambda e, a=rT[0], b2=rT[1], d=kd:
                                   e.tensor_sub(d[:, :, 0:16], a, b2), inc="sv")
                            sch.op("vector", lambda e, a=rT[2], k=kr, c=c16:
                                   e.tensor_tensor(a, k[:, :, 16:32], c,
                                                   ALU.mult), inc="sv")
                            sch.op("vector", lambda e, a=rT[3], k=kr, s=s16:
                                   e.tensor_tensor(a, k[:, :, 0:16], s,
                                                   ALU.mult), inc="sv")
                            sch.op("vector", lambda e, a=rT[2], b2=rT[3], d=kd:
                                   e.tensor_add(d[:, :, 16:32], a, b2), inc="sv")
                            t_b = sch.op("vector", lambda e, k=kr, d=kd:
                                         e.tensor_copy(d[:, :, 32:64],
                                                       k[:, :, 32:64]), inc="sv")
                            r_krst[si2].read(t_b)
                            r_kr[ki].wrote(t_b)
                            kr_done[tcl] = sch.cnt("sv")
                    r_xnfm1.read(sch.cnt("sp"))
                    if tile + 1 < NT_C:
                        tr_C1(tile + 1)
                    for tcl in range(4):
                        ki = tcl
                        tp = tcl % 2
                        tt2 = None
                        for jc in range(8):
                            tt2 = sch.op(
                                "tensor",
                                lambda e, i=ki, jc=jc, tp=tp: e.transpose(
                                    psTr1[:, tp * 1024 + jc * 128:
                                          tp * 1024 + (jc + 1) * 128],
                                    krope_tm[i][:, jc * 128:(jc + 1) * 128],
                                    identbf[:, :]),
                                inc="sp" if jc == 7 else None,
                                waits=([kr_done[tcl]] + r_psTr1[tp].for_write())
                                if jc == 0 else [])
                        r_psTr1[tp].wrote(tt2)
                        r_kr[ki].read(tt2)
                        dst = krope_fm.rearrange("p (jc t) -> p jc t",
                                                 t=512)[
                            :, :, tcl * 128:(tcl + 1) * 128]
                        src = psTr1[:, tp * 1024:(tp + 1) * 1024] \
                            .rearrange("p (b t) -> p b t", t=128)
                        t_cp = sch.op("scalar", lambda e, d=dst, s=src:
                                      e.activation(d, s, AF.Copy), inc="sa",
                                      waits=[tt2] + r_krfm.for_write())
                        r_psTr1[tp].read(t_cp)
                    r_krfm.wrote(sch.cnt("sa"))
                    krope_ready = sch.cnt("sa")

                    for tcl in range(4):
                        oi = 0
                        xi = tcl % 2
                        row = t0 + tcl * 128
                        t_xa = sch.dma("sync", f"dx{xi}",
                                       lambda e, r=row, i=xi: e.dma_start(
                                           xc1[i][:, 0:1024],
                                           d_x[r:r + 128, 0:1024]),
                                       waits=r_x1[xi].for_write())
                        t_x = sch.dma("sync", f"dx{xi + 2}",
                                      lambda e, r=row, i=xi: e.dma_start(
                                          xc1[i][:, 1024:2048],
                                          d_x[r:r + 128, 1024:2048]),
                                      waits=r_x1[xi].for_write())
                        r_x1[xi].wrote(t_x)
                        tt = None
                        seg_tok = {}
                        for ci in range(12):
                            if ci < 4:
                                l_ap = ckv_fm[:, ci * OWN + t0 + tcl * 128:
                                              ci * OWN + t0 + (tcl + 1) * 128]
                                rbase, roff = w2_sb, ci * 2048
                            else:
                                jc = ci - 4
                                l_ap = krope_fm[:, jc * 512 + tcl * 128:
                                                jc * 512 + (tcl + 1) * 128]
                                rbase, roff = wrope_sb, jc * 2048
                            for seg in range(4):
                                w0 = []
                                if ci == 0:
                                    w0 = r_psC4[seg].for_write()
                                    if seg == 0:
                                        w0 = w0 + [krope_ready]
                                tt = sch.op(
                                    "tensor",
                                    lambda e, l=l_ap, rb=rbase, ro=roff, s=seg,
                                    ci=ci: e.matmul(
                                        psC[:, s * 512:(s + 1) * 512], l,
                                        rb[:, ro + s * 512:ro + (s + 1) * 512],
                                        start=(ci == 0), stop=(ci == 11)),
                                    inc="sp" if ci == 11 else None, waits=w0)
                                if ci == 11:
                                    seg_tok[seg] = tt
                        t_f = None
                        for seg in range(4):
                            r_psC4[seg].wrote(seg_tok[seg])
                            t_f = sch.op("vector",
                                         lambda e, i=oi, x=xi, s=seg:
                                         e.tensor_add(
                                             outst[i][:, s * 512:(s + 1) * 512],
                                             xc1[x][:, s * 512:(s + 1) * 512],
                                             psC[:, s * 512:(s + 1) * 512]),
                                         inc="sv",
                                         waits=[seg_tok[seg], t_xa, t_x]
                                         + r_out[oi].for_write())
                            r_psC4[seg].read(t_f)
                        r_x1[xi].read(t_f)
                        r_out[oi].wrote(t_f)
                        t_y = sch.dma("sync", "dy0",
                                      lambda e, r=row, i=oi: e.dma_start(
                                          d_y[r:r + 128, :], outst[i][:, :]),
                                      waits=[t_f])
                        r_out[oi].read(t_y)
                    r_krfm.read(sch.cnt("sp"))

                head_C1(0)
                tr_C1(0)
                head_C1(1)
                for tile in range(NT_C):
                    body_C1(tile)
                    if tile + 2 < NT_C:
                        head_C1(tile + 2)

                sch.op("sync", lambda e: None,
                       waits=[sch.cnt("dy0"), sch.cnt("dy1")])
                sch.flush(nc, semmap)

        abc1.close()
        if phases <= 4:
            return nc
        with ExitStack() as c2ctx:
            csb = lambda n, s, d: c2ctx.enter_context(nc.sbuf_tensor(n, s, d))
            xn_fm2 = csb("xn_fm2", [128, 16 * 512], BF16)
            xn_tm2 = [csb(f"xn_tm2_{i}", [128, 2048], BF16) for i in range(4)]
            f32b = [csb(f"f32b{i}", [128, 2048], F32) for i in range(2)]
            outst2p = [csb(f"outst2_{i}", [128, 2048], F32) for i in range(2)]
            slab = [csb(f"slab{i}", [128, 16 * 512], BF16) for i in range(2)]
            wdn = [csb(f"wdn{i}", [128, 2 * 2048], BF16) for i in range(4)]
            fuse = csb("fuse", [128, 64 * 512], BF16)
            gsil = [csb(f"gsil{i}", [128, 512], BF16) for i in range(4)]

            r_x2 = [Res() for _ in range(2)]
            r_xntm2 = [Res() for _ in range(4)]
            r_xnfm2 = Res()
            r_out2 = [Res(), Res()]
            r_slab = [Res() for _ in range(2)]
            r_wdn = [Res() for _ in range(4)]
            r_fuse = Res()
            r_gsil = [Res() for _ in range(4)]
            r_psDh = [Res(), Res()]
            r_psTr2 = [Res(), Res()]
            r_psG = [Res(), Res()]
            r_psU = [Res(), Res()]
            psD = c2ctx.enter_context(
                nc.psum_tensor("psD", [128, 4096], F32))
            psG = psD[:, 0:1024]
            psU = psD[:, 1024:2048]
            psTr2 = psD[:, 2048:3072].bitcast(BF16)

            head_tn2 = {}
            norm_tok2 = {}
            pf_slab = {}

            t_idf = sch.dma("sync", "dI", lambda e: e.dma_start(
                identf[:, :], d_ident[:, :]))
            t_sgN = sch.dma("sync", "dI", lambda e: e.dma_start(
                f32b[1][0:64, 128:256], d_sg.rearrange("(a p) -> a p", p=128)))
            t_suN = sch.dma("sync", "dI", lambda e: e.dma_start(
                f32b[1][0:64, 256:384], d_su.rearrange("(a p) -> a p", p=128)))
            t_sdr = sch.dma("sync", "dI", lambda e: e.dma_start(
                f32b[0][0:1, :], d_sd.rearrange("(a f) -> a f", a=1)))
            t_ones = sch.op("gpsimd", lambda e: e.memset(
                f32b[1][0:1, 0:128], 1.0), inc="sg")
            t_tr1 = sch.op("tensor", lambda e: e.transpose(
                psD[:, 0:64], f32b[1][0:64, 128:256], identf[0:64, 0:64]),
                inc="sp", waits=[t_sgN, t_idf])
            t_c1 = sch.op("vector", lambda e: e.tensor_copy(
                sg_col[:, :], psD[:, 0:64]), inc="sv", waits=[t_tr1])
            t_tr2 = sch.op("tensor", lambda e: e.transpose(
                psD[:, 64:128], f32b[1][0:64, 256:384], identf[0:64, 0:64]),
                inc="sp", waits=[t_suN])
            t_c2i = sch.op("vector", lambda e: e.tensor_copy(
                su_col[:, :], psD[:, 64:128]), inc="sv", waits=[t_tr2])
            r_psG[0].wrote(t_tr2)
            r_psG[0].read(t_c1)
            r_psG[0].read(t_c2i)
            sd_last = {0: t_tr2, 1: None}
            for seg in range(4):
                sl = seg % 2
                w0 = [t_ones, t_sdr] + r_psG[sl].for_write()
                tt = sch.op("tensor", lambda e, s=seg, sl=sl: e.matmul(
                    psD[:, sl * 512:(sl + 1) * 512], f32b[1][0:1, 0:128],
                    f32b[0][0:1, s * 512:(s + 1) * 512],
                    start=True, stop=True), inc="sp", waits=w0)
                r_psG[sl].wrote(tt)
                tc_ = sch.op("vector", lambda e, s=seg, sl=sl: e.tensor_copy(
                    sdown_rep[:, s * 512:(s + 1) * 512],
                    psD[:, sl * 512:(sl + 1) * 512]), inc="sv", waits=[tt])
                r_psG[sl].read(tc_)
            r_x2[0].wrote(t_sdr)
            r_x2[1].wrote(t_suN)
            for sl in range(2):
                r_x2[0].read(r_psG[sl].w)
                r_x2[1].read(r_psG[sl].w)
            init2_sg, init2_su = t_c1, t_c2i

            def head_C2(tile):
                t0 = tile * TC
                tns = []
                for kc in range(4):
                    xi = kc % 2
                    row = t0 + kc * 128
                    idx = tile * 4 + kc
                    t_xa = sch.dma("sync", f"dx{xi}",
                                   lambda e, r=row, i=xi: e.dma_start(
                                       f32b[i][:, 0:1024],
                                       d_x[r:r + 128, 0:1024]),
                                   waits=r_x2[xi].for_write())
                    t_xb = sch.dma("sync", f"dx{xi + 2}",
                                   lambda e, r=row, i=xi: e.dma_start(
                                       f32b[i][:, 1024:2048],
                                       d_x[r:r + 128, 1024:2048]),
                                   waits=r_x2[xi].for_write())
                    r_x2[xi].wrote(t_xb)
                    t_n = sch.op("gpsimd", lambda e, i=xi, kc=kc, idx=idx:
                                 e.tensor_scalar(
                                     xn_tm2[kc][:, :], f32b[i][:, :],
                                     st_rstd[:, idx:idx + 1],
                                     st_nmr[:, idx:idx + 1],
                                     ALU.mult, ALU.add),
                                 inc="sg",
                                 waits=[t_xa, t_xb]
                                 + r_xntm2[kc].for_write())
                    r_x2[xi].read(t_n)
                    r_xntm2[kc].wrote(t_n)
                    tns.append(t_n)
                norm_tok2[tile] = tns

            xn_toks2 = {}

            def tr_C2(tile):
                for kc in range(4):
                    ni = kc
                    t_n = norm_tok2[tile][kc]
                    for g in range(2):
                        tp = (kc * 2 + g) % 2
                        tt = None
                        for f8 in range(8):
                            fc = g * 8 + f8
                            tt = sch.op(
                                "tensor",
                                lambda e, ni=ni, fc=fc, f8=f8, tp=tp:
                                e.transpose(
                                    psTr2[:, tp * 1024 + f8 * 128:
                                          tp * 1024 + (f8 + 1) * 128],
                                    xn_tm2[ni][:, fc * 128:(fc + 1) * 128],
                                    identbf[:, :]),
                                inc="sp" if f8 == 7 else None,
                                waits=([t_n]
                                       + r_psTr2[tp].for_write()
                                       + r_psDh[1].for_write())
                                if f8 == 0 else [])
                        r_psTr2[tp].wrote(tt)
                        r_xntm2[ni].read(tt)
                        dst = xn_fm2.rearrange("p (fc t) -> p fc t", t=512)[
                            :, g * 8:(g + 1) * 8, kc * 128:(kc + 1) * 128]
                        src_ = psTr2[:, tp * 1024:(tp + 1) * 1024] \
                            .rearrange("p (b t) -> p b t", t=128)
                        t_cp = sch.op("scalar", lambda e, d=dst, s=src_:
                                      e.activation(d, s, AF.Copy), inc="sa",
                                      waits=[tt] + r_xnfm2.for_write())
                        r_psTr2[tp].read(t_cp)
                r_xnfm2.wrote(sch.cnt("sa"))
                xn_toks2[tile] = sch.cnt("sa")

            def body_C2(tile):
                t0 = tile * TC
                xn_ready = xn_toks2[tile]

                for iseg in range(16):
                    c0 = iseg * 512
                    if iseg == 0 and tile in pf_slab:
                        t_lg, t_lu = pf_slab.pop(tile)
                    else:
                        t_lg = sch.dma("gpsimd", "dsl0", lambda e, c0=c0:
                                       e.dma_start(
                                           slab[0].rearrange(
                                               "p (kc i) -> p kc i", i=512),
                                           d_wg[:, c0:c0 + 512].rearrange(
                                               "(kc p) i -> p kc i", p=128)),
                                       waits=r_slab[0].for_write())
                        r_slab[0].wrote(t_lg)
                        t_lu = sch.dma("gpsimd", "dsl1", lambda e, c0=c0:
                                       e.dma_start(
                                           slab[1].rearrange(
                                               "p (kc i) -> p kc i", i=512),
                                           d_wu[:, c0:c0 + 512].rearrange(
                                               "(kc p) i -> p kc i", p=128)),
                                       waits=r_slab[1].for_write())
                        r_slab[1].wrote(t_lu)
                    sil_tok = {}
                    for icl in range(4):
                        ic = iseg * 4 + icl
                        gp = ic % 2
                        tt = None
                        for kk in range(16):
                            w0 = []
                            if kk == 0:
                                w0 = ([xn_ready, t_lg] + r_psG[gp].for_write()
                                      + r_psDh[0].for_write())
                            tt = sch.op(
                                "tensor",
                                lambda e, kk=kk, icl=icl, gp=gp:
                                e.matmul(
                                    psG[:, gp * 512:(gp + 1) * 512],
                                    slab[0][:, kk * 512 + icl * 128:
                                            kk * 512 + (icl + 1) * 128],
                                    xn_fm2[:, kk * 512:(kk + 1) * 512],
                                    start=(kk == 0), stop=(kk == 15)),
                                inc="sp" if kk == 15 else None, waits=w0)
                        r_psG[gp].wrote(tt)
                        r_slab[0].read(tt)
                        t_sil = sch.op(
                            "scalar", lambda e, ic=ic, gp=gp, icl=icl:
                            e.activation(
                                gsil[icl][:, :],
                                psG[:, gp * 512:(gp + 1) * 512], AF.Silu,
                                scale=sg_col[:, ic:ic + 1]),
                            inc="sa", waits=[tt] + r_gsil[icl].for_write()
                            + ([init2_sg] if ic == 0 and tile == 0 else []))
                        r_psG[gp].read(t_sil)
                        r_gsil[icl].wrote(t_sil)
                        sil_tok[icl] = t_sil
                    for icl in range(4):
                        ic = iseg * 4 + icl
                        up = icl % 2
                        tt = None
                        for kk in range(16):
                            w0 = []
                            if kk == 0:
                                w0 = ([t_lu] + r_psU[up].for_write()
                                      + r_psDh[0].for_write())
                            tt = sch.op(
                                "tensor",
                                lambda e, kk=kk, icl=icl, up=up: e.matmul(
                                    psU[:, up * 512:(up + 1) * 512],
                                    slab[1][:, kk * 512 + icl * 128:
                                            kk * 512 + (icl + 1) * 128],
                                    xn_fm2[:, kk * 512:(kk + 1) * 512],
                                    start=(kk == 0), stop=(kk == 15)),
                                inc="sp" if kk == 15 else None, waits=w0)
                        r_psU[up].wrote(tt)
                        r_slab[1].read(tt)
                        t_f = sch.op(
                            "vector", lambda e, ic=ic, icl=icl, up=up:
                            e.scalar_tensor_tensor(
                                fuse[:, ic * 512:(ic + 1) * 512],
                                psU[:, up * 512:(up + 1) * 512],
                                su_col[:, ic:ic + 1],
                                gsil[icl][:, :], ALU.mult, ALU.mult),
                            inc="sv",
                            waits=([tt, sil_tok[icl]] + r_fuse.for_write())
                            if ic == 0 else [tt, sil_tok[icl]])
                        r_psU[up].read(t_f)
                        r_gsil[icl].read(t_f)
                r_xnfm2.read(sch.cnt("sp"))
                r_fuse.wrote(sch.cnt("sv"))
                fuse_ready = sch.cnt("sv")

                if tile + 1 < NT_C:
                    tr_C2(tile + 1)
                sa_prev = sch.cnt("sa")

                if tile + 1 < NT_C:
                    t_pg = sch.dma("gpsimd", "dsl0", lambda e: e.dma_start(
                        slab[0].rearrange("p (kc i) -> p kc i", i=512),
                        d_wg[:, 0:512].rearrange("(kc p) i -> p kc i", p=128)),
                        waits=r_slab[0].for_write())
                    r_slab[0].wrote(t_pg)
                    t_pu = sch.dma("gpsimd", "dsl1", lambda e: e.dma_start(
                        slab[1].rearrange("p (kc i) -> p kc i", i=512),
                        d_wu[:, 0:512].rearrange("(kc p) i -> p kc i", p=128)),
                        waits=r_slab[1].for_write())
                    r_slab[1].wrote(t_pu)
                    pf_slab[tile + 1] = (t_pg, t_pu)

                sring = ["dwd0", "dwd1", "dwd2", "dwd3"]
                ld_tok = {}

                def wdn_load(j):
                    di = j % 4
                    r0 = (j % 32) * 256
                    t = sch.dma(
                        "gpsimd", sring[di],
                        lambda e, d=di, r0=r0: e.dma_start(
                            wdn[d].rearrange("p (c f) -> p c f", f=2048),
                            d_wd[r0:r0 + 256, :].rearrange(
                                "(c p) f -> p c f", p=128)),
                        waits=r_wdn[di].for_write())
                    r_wdn[di].wrote(t)
                    ld_tok[j] = t

                for j in range(3):
                    wdn_load(j)
                for half in range(2):
                    yp_tok = {}
                    for tc2 in range(2):
                        tcl = half * 2 + tc2
                        row = t0 + tcl * 128
                        yi = tcl % 2
                        t_yp = sch.dma("sync", f"dx{yi}",
                                       lambda e, r=row, i=yi: e.dma_start(
                                           f32b[i][:, :], d_y[r:r + 128, :]),
                                       waits=r_x2[yi].for_write())
                        r_x2[yi].wrote(t_yp)
                        yp_tok[tc2] = t_yp
                    tt = None
                    for jl in range(32):
                        jj = half * 32 + jl
                        di = jj % 4
                        t_ld = ld_tok[jj]
                        for c2 in range(2):
                            ic = jl * 2 + c2
                            for tc2 in range(2):
                                tcl = half * 2 + tc2
                                for seg in range(4):
                                    w0 = []
                                    if ic == 0 and seg == 0:
                                        w0 = ([fuse_ready, t_ld, sa_prev]
                                              + r_psDh[tc2].for_write())
                                    elif c2 == 0 and tc2 == 0 and seg == 0:
                                        w0 = [t_ld]
                                    tt = sch.op(
                                        "tensor",
                                        lambda e, d=di, ic=ic, c2=c2, t=tcl,
                                        tc2=tc2, s=seg: e.matmul(
                                            psD[:, tc2 * 2048 + s * 512:
                                                tc2 * 2048 + (s + 1) * 512],
                                            fuse[:, ic * 512 + t * 128:
                                                 ic * 512 + (t + 1) * 128],
                                            wdn[d][:, c2 * 2048 + s * 512:
                                                   c2 * 2048 +
                                                   (s + 1) * 512],
                                            start=(ic == 0),
                                            stop=(ic == 63)),
                                        inc="sp" if (c2 == 1 and tc2 == 1
                                                     and seg == 3)
                                        else None, waits=w0)
                        r_wdn[di].read(tt)
                        if jj + 3 < 64:
                            wdn_load(jj + 3)
                    r_psDh[0].wrote(tt)
                    r_psDh[1].wrote(tt)
                    r_fuse.read(tt)
                    for tc2 in (1, 0):
                        tcl = half * 2 + tc2
                        row = t0 + tcl * 128
                        yi = tcl % 2
                        t_m = sch.op("vector", lambda e, tc2=tc2:
                                     e.tensor_mul(
                                         outst2p[tc2][:, :],
                                         psD[:, tc2 * 2048:(tc2 + 1) * 2048],
                                         sdown_rep[:, :]), inc="sv",
                                     waits=r_psDh[tc2].for_read()
                                     + r_out2[tc2].for_write())
                        r_out2[tc2].wrote(t_m)
                        r_psDh[tc2].read(t_m)
                        t_o = sch.op("vector", lambda e, i=yi, tc2=tc2:
                                     e.tensor_add(
                                         outst2p[tc2][:, :],
                                         outst2p[tc2][:, :], f32b[i][:, :]),
                                     inc="sv", waits=[yp_tok[tc2]])
                        r_x2[yi].read(t_o)
                        r_out2[tc2].wrote(t_o)
                        t_y = sch.dma("sync", f"dy{tc2}",
                                      lambda e, r=row, tc2=tc2: e.dma_start(
                                          d_y[r:r + 128, :],
                                          outst2p[tc2][:, :]),
                                      waits=[t_o])
                        r_out2[tc2].read(t_y)

            head_C2(0)
            tr_C2(0)
            head_C2(1)
            for tile in range(NT_C):
                body_C2(tile)
                if tile + 2 < NT_C:
                    head_C2(tile + 2)

            sch.op("sync", lambda e: None,
                   waits=[sch.cnt("dy0"), sch.cnt("dy1")])
            sch.flush(nc, semmap)

    return nc


_nc_cache = None


def kernel(L0_attn_x, L0_attn_W_dkv, L0_attn_W_dq, L0_attn_W_uk, L0_attn_W_uv,
           L0_attn_W_uq, L0_attn_W_kr, L0_attn_W_qr, L0_attn_W_out,
           L0_ffn_w_gate, L0_ffn_w_up, L0_ffn_w_down, L0_ffn_s_gate,
           L0_ffn_s_up, L0_ffn_s_down):
    global _nc_cache
    x = np.asarray(L0_attn_x, dtype=np.float32)
    weights = dict(
        w_dkv=np.ascontiguousarray(L0_attn_W_dkv, np.float32),
        w_dq=np.ascontiguousarray(L0_attn_W_dq, np.float32),
        w_uk=np.ascontiguousarray(L0_attn_W_uk, np.float32),
        w_uv=np.ascontiguousarray(L0_attn_W_uv, np.float32),
        w_uq=np.ascontiguousarray(L0_attn_W_uq, np.float32),
        w_kr=np.ascontiguousarray(L0_attn_W_kr, np.float32),
        w_qr=np.ascontiguousarray(L0_attn_W_qr, np.float32),
        w_out=np.ascontiguousarray(L0_attn_W_out, np.float32),
        w_gate=np.ascontiguousarray(L0_ffn_w_gate, np.int8),
        w_up=np.ascontiguousarray(L0_ffn_w_up, np.int8),
        w_down=np.ascontiguousarray(L0_ffn_w_down, np.int8),
        s_gate=np.ascontiguousarray(L0_ffn_s_gate, np.float32),
        s_up=np.ascontiguousarray(L0_ffn_s_up, np.float32),
        s_down=np.ascontiguousarray(L0_ffn_s_down, np.float32),
        ident=np.eye(128, dtype=np.float32),
    )
    if _nc_cache is None:
        _nc_cache = build_program()
    nc = _nc_cache

    in_maps = []
    for core in range(8):
        b, half = core // 2, core % 2
        xb = x[b]
        own = xb[half * OWN:(half + 1) * OWN]
        other = xb[(1 - half) * OWN:(2 - half) * OWN]
        m = dict(weights)
        m["xb"] = np.ascontiguousarray(np.concatenate([own, other], axis=0))
        m["posb"] = np.array([[half * OWN, (1 - half) * OWN]], dtype=np.float32)
        in_maps.append(m)

    res = run_bass_kernel_spmd(nc, in_maps, core_ids=list(range(8)))
    out = np.empty((B, S, H), dtype=np.float32)
    for core in range(8):
        b, half = core // 2, core % 2
        out[b, half * OWN:(half + 1) * OWN, :] = res.results[core]["y"]
    return out


# version 25
# speedup vs baseline: 1.1617x; 1.1617x over previous
import math
from contextlib import ExitStack

import numpy as np

import concourse.bass as bass
import concourse.mybir as mybir
from concourse.bass_utils import run_bass_kernel_spmd

F32 = mybir.dt.float32
BF16 = mybir.dt.bfloat16
I8 = mybir.dt.int8
AF = mybir.ActivationFunctionType
ALU = mybir.AluOpType
AXX = mybir.AxisListType.X

B, S, H = 4, 4096, 2048
NH = 16
INTER = 8192
OWN = 2048
TA = 512
NT_A = 8
NT_A_RUN = 4
TC = 512
NT_C = 4
BSUB = 4
TWO_PI = 6.283185307179586
PI = 3.141592653589793
HALF_PI = 1.5707963267948966
NRED = 17
ATTN_SCALE = 1.0 / math.sqrt(128.0)

XS2 = ("dsl2", "dk1", "dk2", "dk3")
ENGINES = ("sync", "gpsimd", "tensor", "vector", "scalar")
DMA_SEMS = ("dI", "dIb", "dwdkv", "dwdq", "dwuv", "dwuqr", "dwuk", "dwkr",
            "dx0", "dx1", "dx2", "dx3", "dk0", "dk1", "dk2", "dk3", "dwo0", "dwo1", "dwo2", "dwo3", "dsl0", "dsl1", "dsl2",
            "dwd0", "dwd1", "dwd2", "dwd3", "dy0", "dy1", "dmc", "scc")
SEMS = ("sp", "sv", "sa", "sg") + DMA_SEMS
SEM_OWNER = {"sp": "tensor", "sv": "vector", "sa": "scalar", "sg": "gpsimd"}


class Sched:
    def __init__(self):
        self.per_engine = {e: [] for e in ENGINES}
        self.counts = {s: 0 for s in SEMS}
        self.hwm = {e: {s: 0 for s in SEMS} for e in ENGINES}

    OWN = {"vector": "sv", "scalar": "sa", "gpsimd": "sg"}

    def op(self, eng, fn, inc=None, incval=1, waits=()):
        ws = []
        own = self.OWN.get(eng)
        if own is not None:
            waits = list(waits) + [(own, self.counts[own])]
        for w in waits:
            if w is None:
                continue
            s, v = w
            if v <= 0 or (eng == "tensor" and s == "sp") \
                    or v <= self.hwm[eng][s]:
                continue
            self.hwm[eng][s] = v
            ws.append((s, v))
        self.per_engine[eng].append((fn, inc, incval, ws))
        if inc is not None:
            self.counts[inc] += incval
            return (inc, self.counts[inc])
        return None

    def cnt(self, sem):
        return (sem, self.counts[sem])

    def dma(self, eng, sem, fn, waits=()):
        prev = (sem, self.counts[sem])
        return self.op(eng, fn, inc=sem, incval=16,
                       waits=list(waits) + [prev])

    def flush(self, nc, semmap):
        with nc.Block() as blk:
            for eng in ENGINES:
                ops = self.per_engine[eng]
                if not ops:
                    continue

                def make_body(ops=ops):
                    def body(e):
                        for fn, inc, incval, ws in ops:
                            for (s, v) in ws:
                                e.wait_ge(semmap[s], v)
                            ins = fn(e)
                            if inc is not None:
                                ins.then_inc(semmap[inc], incval)
                    return body

                getattr(blk, eng)(make_body())
        self.per_engine = {e: [] for e in ENGINES}


class Res:

    def __init__(self):
        self.w = None
        self.rs = []

    def wrote(self, tok):
        self.w = tok
        self.rs = []

    def read(self, tok):
        if tok is not None:
            self.rs.append(tok)

    def for_read(self):
        return [self.w] if self.w else []

    def for_write(self):
        out = list(self.rs)
        if self.w:
            out.append(self.w)
        return out


def build_program(phases=5):
    nc = bass.Bass("TRN2", target_bir_lowering=False)

    d_x = nc.dram_tensor("xb", [S, H], F32, kind="ExternalInput").ap()
    d_wdkv = nc.dram_tensor("w_dkv", [H, 512], F32, kind="ExternalInput").ap()
    d_wdq = nc.dram_tensor("w_dq", [H, 512], F32, kind="ExternalInput").ap()
    d_wuk = nc.dram_tensor("w_uk", [512, 1024], F32, kind="ExternalInput").ap()
    d_wuv = nc.dram_tensor("w_uv", [512, H], F32, kind="ExternalInput").ap()
    d_wuq = nc.dram_tensor("w_uq", [512, 1024], F32, kind="ExternalInput").ap()
    d_wkr = nc.dram_tensor("w_kr", [H, 1024], F32, kind="ExternalInput").ap()
    d_wqr = nc.dram_tensor("w_qr", [512, 1024], F32, kind="ExternalInput").ap()
    d_wout = nc.dram_tensor("w_out", [H, H], F32, kind="ExternalInput").ap()
    d_wg = nc.dram_tensor("w_gate", [H, INTER], I8, kind="ExternalInput").ap()
    d_wu = nc.dram_tensor("w_up", [H, INTER], I8, kind="ExternalInput").ap()
    d_wd = nc.dram_tensor("w_down", [INTER, H], I8, kind="ExternalInput").ap()
    d_sg = nc.dram_tensor("s_gate", [INTER], F32, kind="ExternalInput").ap()
    d_su = nc.dram_tensor("s_up", [INTER], F32, kind="ExternalInput").ap()
    d_sd = nc.dram_tensor("s_down", [H], F32, kind="ExternalInput").ap()
    d_ident = nc.dram_tensor("ident", [128, 128], F32, kind="ExternalInput").ap()
    d_posb = nc.dram_tensor("posb", [1, 2], F32, kind="ExternalInput").ap()
    d_y = nc.dram_tensor("y", [OWN, H], F32, kind="ExternalOutput").ap()
    d_mci = nc.dram_tensor("m_cc_in", [128, 2048], BF16).ap()
    d_mco = nc.dram_tensor("m_cc_out", [128, 2048], BF16).ap()

    sch = Sched()
    ctx = ExitStack()
    with ctx:
        sb = lambda n, s, d: ctx.enter_context(nc.sbuf_tensor(n, s, d))
        semmap = {s: ctx.enter_context(nc.semaphore(s)) for s in SEMS}

        identbf = sb("identbf", [128, 128], BF16)
        identf = sb("identf", [128, 128], F32)
        st_rstd = sb("st_rstd", [128, 32], F32)
        st_nmr = sb("st_nmr", [128, 32], F32)
        st_sum = sb("st_sum", [128, 32], F32)
        st_sq = sb("st_sq", [128, 32], F32)
        st_mean = sb("st_mean", [128, 32], F32)
        st_m2 = sb("st_m2", [128, 32], F32)
        st_var = sb("st_var", [128, 32], F32)
        sdown_rep = sb("sdown_rep", [128, H], F32)
        sg_col = sb("sg_col", [128, 64], F32)
        su_col = sb("su_col", [128, 64], F32)
        cols = sb("cols", [128, 8], F32)
        posb_sb = sb("posb_sb", [1, 2], F32)

        COL_EPS = cols[:, 0:1]
        COL_PI = cols[:, 1:2]
        COL_HPI = cols[:, 2:3]
        COL_NEG1 = cols[:, 3:4]
        COL_LN40 = cols[:, 4:5]
        COL_LNIF = cols[:, 5:6]

        r_tab = Res()
        r_stats = Res()

        abc1 = ExitStack()
        cosA = abc1.enter_context(nc.sbuf_tensor("cosA", [128, 512], F32))
        sinA = abc1.enter_context(nc.sbuf_tensor("sinA", [128, 512], F32))
        ckv_fm = abc1.enter_context(
            nc.sbuf_tensor("ckv_fm", [128, 4 * OWN], BF16))
        Msb = abc1.enter_context(nc.sbuf_tensor("Msb", [128, 2048], F32))

        if phases <= 1:
            return nc
        with ExitStack() as actx:
            asb = lambda n, s, d: actx.enter_context(nc.sbuf_tensor(n, s, d))
            psA = actx.enter_context(nc.psum_tensor("psA", [128, 1024], F32))
            psTrA = actx.enter_context(nc.psum_tensor("psTrA", [128, 2048], BF16))
            psM = actx.enter_context(nc.psum_tensor("psM", [128, 2048], F32))

            wdkv = asb("wdkv", [128, 16 * 512], BF16)
            wdq = asb("wdq", [128, 16 * 512], BF16)
            wuv = asb("wuv", [128, 4 * 2048], BF16)
            wuqr = asb("wuqr", [128, 4 * 2048], BF16)
            xch = [asb(f"xch{i}", [128, 2048], F32) for i in range(4)]
            dump = asb("dump", [128, 2048], BF16)
            xn_fm = asb("xn_fm", [128, 16 * 512], BF16)
            cq_fm = asb("cq_fm", [128, 4 * 512], BF16)
            ckv_sc = None
            v_tm = [asb(f"v_tm{i}", [128, 2048], BF16) for i in range(2)]
            q_tm = [asb(f"q_tm{i}", [128, 2048], BF16) for i in range(2)]
            ropeT = asb("ropeT", [128, 1024], F32)

            r_x = [Res() for _ in range(4)]
            r_xnfm = Res()
            r_cq = Res()
            r_ckvsc = Res()
            r_v = [Res() for _ in range(2)]
            r_q = [Res() for _ in range(2)]
            r_psA = [Res() for _ in range(2)]
            r_psTrA = [Res(), Res()]
            r_psM = Res()
            r_M = Res()
            psA_slot = [psA[:, i * 512:(i + 1) * 512] for i in range(2)]
            rot = [0]

            def next_psA():
                i = rot[0] % 2
                rot[0] += 1
                return i, psA_slot[i]

            qr_st = [asb(f"qr_st{i}", [128, 1024], F32) for i in range(2)]
            xn_tm4 = [asb(f"xn_tm4_{i}", [128, 2048], BF16) for i in range(4)]
            jrow = asb("jrow", [1, 16], F32)
            invf = asb("invf", [1, 16], F32)
            r_qst = [Res(), Res()]
            r_xntm4 = [Res() for _ in range(4)]
            head_tn = {}
            norm_tok = {}

            angle = ropeT[:, 0:256]
            acc = ropeT[:, 256:512]
            mred = ropeT[:, 512:768]
            t_posb = sch.dma("sync", "dI", lambda e: e.dma_start(
                posb_sb[:, :], d_posb[:, :]))
            t_idb = sch.dma("gpsimd", "dIb", lambda e: e.dma_start(
                identbf[:, :], d_ident[:, :]))
            t_cols = None
            for i, val in [(0, 1e-5), (1, PI), (2, HALF_PI), (3, -1.0),
                           (4, -math.log(40.0)), (5, -math.log(10000.0) / 16.0)]:
                t_cols = sch.op("gpsimd",
                                lambda e, i=i, v=val: e.memset(cols[:, i:i + 1], v),
                                inc="sg")
            t_j = sch.op("gpsimd", lambda e: e.iota(
                jrow[:, :], [[1, 16]], base=0, channel_multiplier=0,
                allow_small_or_imprecise_dtypes=True), inc="sg")
            t_invf = sch.op("scalar", lambda e: e.activation(
                invf[:, :], jrow[:, :], AF.Exp,
                bias=COL_LN40[0:1, :], scale=COL_LNIF[0:1, :]),
                inc="sa", waits=[t_j, t_cols])
            t_add = [None, None]
            for i in range(2):
                t_io = sch.op("gpsimd", lambda e, i=i: e.iota(
                    qr_st[i][0:1, 0:1024], [[1, 1024]], base=i * 1024,
                    channel_multiplier=0,
                    allow_small_or_imprecise_dtypes=True), inc="sg")
                t_add[i] = sch.op("vector", lambda e, i=i: e.tensor_scalar_add(
                    qr_st[i][0:1, 0:1024], qr_st[i][0:1, 0:1024],
                    posb_sb[0:1, 0:1]), inc="sv", waits=[t_io, t_posb])
                r_qst[i].wrote(t_add[i])
            t_mm = None
            for bc in range(16):
                qi, off = bc // 8, (bc % 8) * 128
                t_mm = sch.op("tensor", lambda e, qi=qi, off=off, bc=bc:
                              e.matmul(psA[:, bc * 16:(bc + 1) * 16],
                                       qr_st[qi][0:1, off:off + 128],
                                       invf[:, :], start=True, stop=True),
                              inc="sp" if bc == 15 else None,
                              waits=[t_add[0], t_add[1], t_invf]
                              if bc == 0 else [])
            r_psA[0].wrote(t_mm)
            r_qst[0].read(t_mm)
            r_qst[1].read(t_mm)

            def emit_tables():
                t_angle = sch.op("vector", lambda e: e.tensor_copy(
                    angle, psA[:, 0:256]), inc="sv", waits=[t_mm])
                r_psA[0].read(t_angle)

                def cascade(shift, out_tile, bias_col, prev_act):
                    t = sch.op("vector", lambda e: e.tensor_single_scalar(
                        acc, angle, TWO_PI - shift, ALU.is_ge),
                        inc="sv", waits=[t_angle, prev_act])
                    for k in range(2, NRED + 1):
                        t = sch.op("vector",
                                   lambda e, k=k: e.scalar_tensor_tensor(
                                       acc, angle, TWO_PI * k - shift, acc,
                                       ALU.is_ge, ALU.add), inc="sv")
                    t = sch.op("vector", lambda e: e.scalar_tensor_tensor(
                        mred, acc, -TWO_PI, angle, ALU.mult,
                        ALU.add), inc="sv")
                    return sch.op("scalar", lambda e, o=out_tile, b=bias_col:
                                  e.activation(o, mred, AF.Sin, bias=b[:, :],
                                               scale=COL_NEG1[:, :]),
                                  inc="sa", waits=[t, t_cols])

                t_sin = cascade(0.0, sinA[:, 0:256], COL_PI, None)
                t_cos = cascade(HALF_PI, cosA[:, 0:256], COL_HPI, t_sin)
                r_tab.wrote(t_cos)

            t_wdkv = sch.dma("gpsimd", "dwdkv", lambda e: e.dma_start(
                wdkv.rearrange("p (kc c) -> p kc c", c=512),
                d_wdkv.rearrange("(kc p) c -> p kc c", p=128)))
            t_wdq = sch.dma("gpsimd", "dwdq", lambda e: e.dma_start(
                wdq.rearrange("p (kc c) -> p kc c", c=512),
                d_wdq.rearrange("(kc p) c -> p kc c", p=128)))
            t_wuv = sch.dma("gpsimd", "dwuv", lambda e: e.dma_start(
                wuv.rearrange("p (cc f) -> p cc f", f=2048),
                d_wuv.rearrange("(cc p) f -> p cc f", p=128)))
            t_wuq = sch.dma("gpsimd", "dwuqr", lambda e: e.dma_start(
                wuqr.rearrange("p (cc two f) -> p cc two f", two=2, f=1024)
                [:, :, 0, :],
                d_wuq.rearrange("(cc p) f -> p cc f", p=128)))
            t_wqr = sch.dma("gpsimd", "dk0", lambda e: e.dma_start(
                wuqr.rearrange("p (cc two f) -> p cc two f", two=2, f=1024)
                [:, :, 1, :],
                d_wqr.rearrange("(cc p) f -> p cc f", p=128)))

            def head_A(tile):
                t0 = tile * TA
                bc0 = tile * 4
                sum_toks = []
                for kc in range(4):
                    xi = kc
                    bc = bc0 + kc
                    row = t0 + kc * 128
                    t_xa = sch.dma("sync", f"dx{xi}",
                                   lambda e, r=row, i=xi: e.dma_start(
                                       xch[i][:, 0:1024],
                                       d_x[r:r + 128, 0:1024]),
                                   waits=r_x[xi].for_write())
                    t_xb = sch.dma("sync", XS2[xi],
                                   lambda e, r=row, i=xi: e.dma_start(
                                       xch[i][:, 1024:2048],
                                       d_x[r:r + 128, 1024:2048]),
                                   waits=r_x[xi].for_write())
                    r_x[xi].wrote(t_xb)
                    t_s = sch.op("vector", lambda e, i=xi, bc=bc: e.reduce_sum(
                        st_sum[:, bc:bc + 1], xch[i][:, :], AXX),
                        inc="sv", waits=[t_xa, t_xb])
                    t_q2 = sch.op("scalar", lambda e, i=xi, bc=bc: e.activation(
                        dump[:, :], xch[i][:, :], AF.Square,
                        accum_out=st_sq[:, bc:bc + 1]), inc="sa",
                        waits=[t_xa, t_xb])
                    r_x[xi].read(t_s)
                    r_x[xi].read(t_q2)
                    sum_toks += [t_s, t_q2]
                sl = slice(bc0, bc0 + 4)
                sch.op("vector", lambda e, sl=sl: e.tensor_scalar_mul(
                    st_mean[:, sl], st_sum[:, sl], 1.0 / H), inc="sv",
                    waits=sum_toks)
                sch.op("vector", lambda e, sl=sl: e.tensor_scalar_mul(
                    st_var[:, sl], st_sq[:, sl], 1.0 / H), inc="sv")
                sch.op("vector", lambda e, sl=sl: e.tensor_mul(
                    st_m2[:, sl], st_mean[:, sl], st_mean[:, sl]), inc="sv")
                t4 = sch.op("vector", lambda e, sl=sl: e.tensor_sub(
                    st_var[:, sl], st_var[:, sl], st_m2[:, sl]), inc="sv")
                t5 = sch.op("scalar", lambda e, sl=sl: e.activation(
                    st_var[:, sl], st_var[:, sl], AF.Sqrt, bias=COL_EPS[:, :]),
                    inc="sa", waits=[t4, t_cols])
                t6 = sch.op("vector", lambda e, sl=sl: e.reciprocal(
                    st_rstd[:, sl], st_var[:, sl]), inc="sv", waits=[t5])
                t7 = sch.op("vector", lambda e, sl=sl: e.scalar_tensor_tensor(
                    st_nmr[:, sl], st_mean[:, sl], -1.0, st_rstd[:, sl],
                    ALU.mult, ALU.mult), inc="sv")
                if tile < 4:
                    r_stats.wrote(t7)
                head_tn[tile] = (t6, t7)

            def norm_A(tile):
                t6, t7 = head_tn[tile]
                bc0 = tile * 4
                tns = []
                for kc in range(4):
                    xi = kc
                    bc = bc0 + kc
                    t_n = sch.op("gpsimd", lambda e, i=xi, kc=kc, bc=bc:
                                 e.tensor_scalar(
                                     xn_tm4[kc][:, :], xch[i][:, :],
                                     st_rstd[:, bc:bc + 1],
                                     st_nmr[:, bc:bc + 1],
                                     ALU.mult, ALU.add),
                                 inc="sg",
                                 waits=[t7, t6] + r_xntm4[kc].for_write())
                    r_x[xi].read(t_n)
                    r_xntm4[kc].wrote(t_n)
                    tns.append(t_n)
                norm_tok[tile] = tns

            g_toks = {}

            def tr_A(tile):
                g_tok = {}
                for g in range(2):
                    for kc in range(4):
                        ni = kc
                        t_n = norm_tok[tile][kc]
                        tp = (g * 4 + kc) % 2
                        tt = None
                        for f8 in range(8):
                            fc = g * 8 + f8
                            tt = sch.op("tensor",
                                        lambda e, ni=ni, fc=fc, f8=f8, tp=tp:
                                        e.transpose(
                                            psTrA[:, tp * 1024 + f8 * 128:
                                                  tp * 1024 + (f8 + 1) * 128],
                                            xn_tm4[ni][:, fc * 128:(fc + 1) * 128],
                                            identbf[:, :]),
                                        inc="sp" if f8 == 7 else None,
                                        waits=([t_n, t_idb]
                                               + r_psTrA[tp].for_write())
                                        if f8 == 0 else [])
                        r_psTrA[tp].wrote(tt)
                        r_xntm4[ni].read(tt)
                        dst = xn_fm.rearrange("p (fc t) -> p fc t", t=512)[
                            :, g * 8:(g + 1) * 8, kc * 128:(kc + 1) * 128]
                        src_ = psTrA[:, tp * 1024:(tp + 1) * 1024] \
                            .rearrange("p (b t) -> p b t", t=128)
                        if tp == 0:
                            t_cp = sch.op("scalar", lambda e, d=dst, s=src_:
                                          e.activation(d, s, AF.Copy),
                                          inc="sa",
                                          waits=[tt] + r_xnfm.for_write())
                        else:
                            t_cp = sch.op("vector", lambda e, d=dst, s=src_:
                                          e.tensor_copy(d, s), inc="sv",
                                          waits=[tt] + r_xnfm.for_write())
                        r_psTrA[tp].read(t_cp)
                    g_tok[g] = [sch.cnt("sa"), sch.cnt("sv")]
                r_xnfm.wrote(sch.cnt("sv"))
                r_xnfm.read(sch.cnt("sa"))
                g_toks[tile] = g_tok

            def body_A(tile):
                own = tile < 4
                t0 = tile * TA
                bc0 = tile * 4
                g_tok = g_toks[tile]

                for which in range(2):
                    wt = wdkv if which == 0 else wdq
                    t_w = t_wdkv if which == 0 else t_wdq
                    for cc in range(4):
                        si, slot = next_psA()
                        tt = None
                        for kk in range(16):
                            w0 = []
                            if kk == 0:
                                w0 = g_tok[0] + [t_w] + r_psA[si].for_write()
                            elif kk == 8:
                                w0 = list(g_tok[1])
                            tt = sch.op("tensor",
                                        lambda e, w=wt, kk=kk, cc=cc, sl=slot:
                                        e.matmul(sl,
                                                 w[:, kk * 512 + cc * 128:
                                                   kk * 512 + (cc + 1) * 128],
                                                 xn_fm[:, kk * 512:(kk + 1) * 512],
                                                 start=(kk == 0), stop=(kk == 15)),
                                        inc="sp" if kk == 15 else None,
                                        waits=w0)
                        r_psA[si].wrote(tt)
                        if which == 0:
                            if own:
                                dst = ckv_fm[:, cc * OWN + t0:cc * OWN + t0 + 512]
                                rr = None
                            else:
                                dst = ckv_sc[:, cc * 512:(cc + 1) * 512]
                                rr = r_ckvsc
                        else:
                            dst = cq_fm[:, cc * 512:(cc + 1) * 512]
                            rr = r_cq
                        t_e = sch.op("vector", lambda e, d=dst, sl=slot:
                                     e.tensor_copy(d, sl), inc="sv",
                                     waits=[tt] + (rr.for_write() if rr else []))
                        r_psA[si].read(t_e)
                        if rr:
                            rr.wrote(t_e)
                ckv_ready = sch.cnt("sv")
                r_xnfm.read(sch.cnt("sp"))
                if tile + 1 < NT_A_RUN:
                    tr_A(tile + 1)

                vq_tok = {}

                def emit_M(tcl):
                    vb = tcl % 2
                    v_ready, q_ready = vq_tok[tcl]
                    tt = None
                    for h in range(NH):
                        first = (tcl == 0 and (h % 4) == 0)
                        lastb = (tcl == 3 and (h % 4) == 3)
                        w0 = []
                        if h == 0:
                            w0 = q_ready + list(v_ready)
                            if tcl == 0:
                                w0 = w0 + r_psM.for_write()
                        tt = sch.op("tensor", lambda e, h=h, f=first, l=lastb,
                                    i=vb: e.matmul(
                                        psM[:, h * 128:(h + 1) * 128],
                                        v_tm[i][:, h * 128:(h + 1) * 128],
                                        q_tm[i][:, h * 128:(h + 1) * 128],
                                        start=f, stop=l),
                                    inc="sp" if h == 15 else None, waits=w0)
                    r_v[vb].read(tt)
                    r_q[vb].read(tt)
                    if tcl == 3:
                        r_psM.wrote(tt)
                        if tile == 0:
                            t_m = sch.op("vector", lambda e: e.tensor_copy(
                                Msb[:, :], psM[:, :]), inc="sv", waits=[tt])
                        else:
                            t_m = sch.op("vector", lambda e: e.tensor_add(
                                Msb[:, :], Msb[:, :], psM[:, :]), inc="sv",
                                waits=[tt])
                        r_psM.read(t_m)
                        r_M.wrote(t_m)

                for tcl in range(4):
                    bc = bc0 + tcl
                    vb = tcl % 2
                    for seg in range(4):
                        si, slot = next_psA()
                        tt = None
                        for cc in range(4):
                            if own:
                                l_ap = ckv_fm[:, cc * OWN + t0 + tcl * 128:
                                              cc * OWN + t0 + (tcl + 1) * 128]
                            else:
                                l_ap = ckv_sc[:, cc * 512 + tcl * 128:
                                              cc * 512 + (tcl + 1) * 128]
                            tt = sch.op("tensor",
                                        lambda e, l=l_ap, cc=cc, s=seg, sl=slot:
                                        e.matmul(sl, l,
                                                 wuv[:, cc * 2048 + s * 512:
                                                     cc * 2048 + (s + 1) * 512],
                                                 start=(cc == 0), stop=(cc == 3)),
                                        inc="sp" if cc == 3 else None,
                                        waits=([ckv_ready, t_wuv]
                                               + r_psA[si].for_write())
                                        if cc == 0 else [])
                        r_psA[si].wrote(tt)
                        t_e = sch.op("scalar", lambda e, s=seg, i=vb, sl=slot:
                                     e.activation(
                                         v_tm[i][:, s * 512:(s + 1) * 512], sl,
                                         AF.Copy), inc="sa",
                                     waits=[tt] + r_v[vb].for_write())
                        r_psA[si].read(t_e)
                    r_v[vb].wrote(sch.cnt("sa"))
                    v_ready = [sch.cnt("sa")]

                    for seg in range(4):
                        si, slot = next_psA()
                        tt = None
                        for cc in range(4):
                            tt = sch.op("tensor",
                                        lambda e, cc=cc, s=seg, sl=slot: e.matmul(
                                            sl,
                                            cq_fm[:, cc * 512 + tcl * 128:
                                                  cc * 512 + (tcl + 1) * 128],
                                            wuqr[:, cc * 2048 + s * 512:
                                                 cc * 2048 + (s + 1) * 512],
                                            start=(cc == 0), stop=(cc == 3)),
                                        inc="sp" if cc == 3 else None,
                                        waits=([ckv_ready, t_wuq, t_wqr]
                                               + r_psA[si].for_write())
                                        if cc == 0 else [])
                        r_psA[si].wrote(tt)
                        if seg < 2:
                            src = slot.rearrange("p (h d) -> p h d", d=64)
                            dst = q_tm[vb].rearrange("p (h d) -> p h d", d=128)[
                                :, seg * 8:(seg + 1) * 8, 0:64]
                            t_e = sch.op("scalar", lambda e, s=src, dd=dst:
                                         e.activation(dd, s, AF.Copy), inc="sa",
                                         waits=[tt] + r_q[vb].for_write())
                            r_psA[si].read(t_e)
                            r_q[vb].wrote(t_e)
                        else:
                            soff = (seg - 2) * 512
                            t_qe = sch.op("scalar", lambda e, sl=slot, i=vb,
                                          so=soff: e.activation(
                                              qr_st[i][:, so:so + 512], sl,
                                              AF.Copy), inc="sa",
                                          waits=[tt] + r_qst[vb].for_write())
                            r_psA[si].read(t_qe)
                            r_qst[vb].wrote(t_qe)
                            h0 = (seg - 2) * 8
                            qr = qr_st[vb][:, soff:soff + 512] \
                                .rearrange("p (h d) -> p h d", d=64)
                            qq = q_tm[vb].rearrange("p (h d) -> p h d", d=128)
                            c16 = cosA[:, bc * 16:bc * 16 + 16].unsqueeze(1) \
                                .broadcast_to([128, 8, 16])
                            s16 = sinA[:, bc * 16:bc * 16 + 16].unsqueeze(1) \
                                .broadcast_to([128, 8, 16])
                            rT = [ropeT.rearrange("p (s t) -> p s t", t=256)
                                  [:, i, :].rearrange("p (h j) -> p h j", j=16)
                                  [:, 0:8, :] for i in range(4)]
                            w0 = [t_qe, r_tab.w] + r_q[vb].for_write()
                            sch.op("vector", lambda e, a=rT[0], q=qr, c=c16:
                                   e.tensor_tensor(a, q[:, :, 0:16], c, ALU.mult),
                                   inc="sv", waits=w0)
                            sch.op("vector", lambda e, a=rT[1], q=qr, s=s16:
                                   e.tensor_tensor(a, q[:, :, 16:32], s, ALU.mult),
                                   inc="sv")
                            sch.op("vector", lambda e, a=rT[0], b2=rT[1], d=qq,
                                   h0=h0: e.tensor_sub(d[:, h0:h0 + 8, 64:80],
                                                       a, b2), inc="sv")
                            sch.op("vector", lambda e, a=rT[2], q=qr, c=c16:
                                   e.tensor_tensor(a, q[:, :, 16:32], c, ALU.mult),
                                   inc="sv")
                            sch.op("vector", lambda e, a=rT[3], q=qr, s=s16:
                                   e.tensor_tensor(a, q[:, :, 0:16], s, ALU.mult),
                                   inc="sv")
                            sch.op("vector", lambda e, a=rT[2], b2=rT[3], d=qq,
                                   h0=h0: e.tensor_add(d[:, h0:h0 + 8, 80:96],
                                                       a, b2), inc="sv")
                            tg = sch.op("vector", lambda e, q=qr, d=qq, h0=h0:
                                        e.tensor_copy(d[:, h0:h0 + 8, 96:128],
                                                      q[:, :, 32:64]), inc="sv")
                            r_qst[vb].read(tg)
                            r_q[vb].wrote(tg)
                    vq_tok[tcl] = (v_ready, [sch.cnt("sa"), sch.cnt("sv")])
                    if tcl >= 1:
                        emit_M(tcl - 1)
                emit_M(3)
                pe_now = sch.cnt("sp")
                r_cq.read(pe_now)
                r_ckvsc.read(pe_now)

            head_A(0)
            norm_A(0)
            head_A(1)
            emit_tables()
            tr_A(0)
            for tile in range(NT_A_RUN):
                if tile + 1 < NT_A_RUN:
                    norm_A(tile + 1)
                body_A(tile)
                if tile + 2 < NT_A_RUN:
                    head_A(tile + 2)

            msb_bf = Msb.bitcast(BF16)[:, 0:2048]
            t_sc = sch.op("vector", lambda e: e.tensor_scalar_mul(
                msb_bf, Msb[:, :], ATTN_SCALE), inc="sv",
                waits=r_M.for_read())
            r_M.read(t_sc)
            t_up = sch.dma("gpsimd", "dmc", lambda e: e.dma_start(
                d_mci[:, :], msb_bf), waits=[t_sc])
            t_cc = sch.op("gpsimd", lambda e: e.collective_compute(
                "AllReduce", ALU.add,
                replica_groups=[[0, 1], [2, 3], [4, 5], [6, 7]],
                ins=[d_mci.opt()], outs=[d_mco.opt()]),
                inc="scc", incval=1, waits=[t_up])
            mex_tok = sch.dma("gpsimd", "dmc", lambda e: e.dma_start(
                msb_bf, d_mco[:, :]), waits=[t_cc])
            r_M.wrote(mex_tok)

            sch.flush(nc, semmap)

        if phases <= 2:
            return nc
        with ExitStack() as wctx:
            wsb = lambda n, s, d: wctx.enter_context(nc.sbuf_tensor(n, s, d))
            w2_sb = wsb("w2_sb", [128, 4 * 2048], BF16)
            wrope_sb = wsb("wrope_sb", [128, 8 * 2048], BF16)

            with ExitStack() as bctx:
                bsb = lambda n, s, d: bctx.enter_context(nc.sbuf_tensor(n, s, d))
                psB = bctx.enter_context(nc.psum_tensor("psB", [128, 2048], F32))
                psTrB = bctx.enter_context(
                    nc.psum_tensor("psTrB", [128, 1024], BF16))
                Mt_bf = bsb("Mt_bf", [128, 2048], BF16)
                wout_b = [bsb(f"wout_b{i}", [128, 2048], BF16) for i in range(4)]
                wuk_bf = bsb("wuk_bf", [128, 4 * 1024], BF16)
                wukT = bsb("wukT", [128, 8 * 512], BF16)
                wc_st = bsb("wc_st", [128, 8 * 2048], BF16)

                t_mt = sch.op("vector", lambda e: e.tensor_copy(
                    Mt_bf[:, :], msb_bf), inc="sv", waits=[mex_tok])
                t_wuk = sch.dma("gpsimd", "dwuk", lambda e: e.dma_start(
                    wuk_bf.rearrange("p (cc f) -> p cc f", f=1024),
                    d_wuk.rearrange("(cc p) f -> p cc f", p=128)))

                r_psTrB = Res()
                for ph in range(8 if BSUB >= 2 else 0):
                    tt = None
                    for hp in range(2):
                        h = 2 * ph + hp
                        for cc2 in range(4):
                            slot = psTrB[hp * 64:hp * 64 + 64,
                                         cc2 * 128:(cc2 + 1) * 128]
                            tt = sch.op("tensor",
                                        lambda e, h=h, c=cc2, sl=slot:
                                        e.transpose(
                                            sl,
                                            wuk_bf[:, c * 1024 + h * 64:
                                                   c * 1024 + (h + 1) * 64],
                                            identbf[:, :]),
                                        inc="sp" if (hp == 1 and cc2 == 3)
                                        else None,
                                        waits=([t_wuk] + r_psTrB.for_write())
                                        if (hp == 0 and cc2 == 0) else [])
                    r_psTrB.wrote(tt)
                    t_cp = sch.op("vector", lambda e, ph=ph: e.tensor_copy(
                        wukT[:, ph * 512:(ph + 1) * 512], psTrB[:, 0:512]),
                        inc="sv", waits=[tt])
                    r_psTrB.read(t_cp)
                wukT_ready = sch.cnt("sv")

                r_psBs = [Res() for _ in range(4)]
                r_woutb = [Res() for _ in range(4)]
                for h in range(NH if BSUB >= 3 else 0):
                    wi = h % 4
                    t_w = sch.dma("gpsimd", f"dwo{wi}",
                                  lambda e, h=h, i=wi: e.dma_start(
                                      wout_b[i][:, :],
                                      d_wout[h * 128:(h + 1) * 128, :]),
                                  waits=r_woutb[wi].for_write())
                    r_woutb[wi].wrote(t_w)
                    for seg in range(4):
                        tt = sch.op("tensor", lambda e, h=h, s=seg, i=wi:
                                    e.matmul(psB[:, s * 512:(s + 1) * 512],
                                             Mt_bf[:, h * 128:(h + 1) * 128],
                                             wout_b[i][:, s * 512:(s + 1) * 512],
                                             start=True, stop=True),
                                    inc="sp",
                                    waits=([t_w, t_mt] if seg == 0 else [])
                                    + r_psBs[seg].for_write())
                        r_psBs[seg].wrote(tt)
                        d1 = wc_st[(h % 2) * 64:(h % 2) * 64 + 64,
                                   (h // 2) * 2048 + seg * 512:
                                   (h // 2) * 2048 + (seg + 1) * 512]
                        d2 = wrope_sb[(h % 2) * 64:(h % 2) * 64 + 64,
                                      (h // 2) * 2048 + seg * 512:
                                      (h // 2) * 2048 + (seg + 1) * 512]
                        t_e1 = sch.op("scalar", lambda e, d=d1, s=seg:
                                      e.activation(
                                          d, psB[0:64, s * 512:(s + 1) * 512],
                                          AF.Copy), inc="sa", waits=[tt])
                        t_e2 = sch.op("vector", lambda e, d=d2, s=seg:
                                      e.tensor_copy(
                                          d, psB[64:128,
                                                 s * 512:(s + 1) * 512]),
                                      inc="sv", waits=[tt])
                        r_psBs[seg].read(t_e1)
                        r_psBs[seg].read(t_e2)
                    r_woutb[wi].read(sch.cnt("sp"))
                wc_ready = [sch.cnt("sv"), sch.cnt("sa")]

                for cc in range(4 if BSUB >= 4 else 0):
                    tt = None
                    for seg in range(4):
                        for ph in range(8):
                            l_ap = wukT[:, ph * 512 + cc * 128:
                                        ph * 512 + (cc + 1) * 128]
                            rh = wc_st[:, ph * 2048 + seg * 512:
                                       ph * 2048 + (seg + 1) * 512]
                            tt = sch.op("tensor",
                                        lambda e, l=l_ap, r=rh, s=seg, ph=ph:
                                        e.matmul(psB[:, s * 512:(s + 1) * 512],
                                                 l, r, start=(ph == 0),
                                                 stop=(ph == 7)),
                                        inc="sp" if (ph == 7 and seg == 3)
                                        else None,
                                        waits=(list(wc_ready) + [wukT_ready]
                                               + r_psBs[0].for_write()
                                               + r_psBs[1].for_write()
                                               + r_psBs[2].for_write()
                                               + r_psBs[3].for_write())
                                        if (ph == 0 and seg == 0) else [])
                    for s4 in range(4):
                        r_psBs[s4].wrote(tt)
                    t_e = sch.op("vector", lambda e, cc=cc: e.tensor_copy(
                        w2_sb[:, cc * 2048:(cc + 1) * 2048], psB[:, :]),
                        inc="sv", waits=[tt])
                    for s4 in range(4):
                        r_psBs[s4].read(t_e)

                sch.flush(nc, semmap)

            if phases <= 3:
                return nc
            with ExitStack() as c1ctx:
                csb = lambda n, s, d: c1ctx.enter_context(nc.sbuf_tensor(n, s, d))
                psC = c1ctx.enter_context(nc.psum_tensor("psC", [128, 2048], F32))
                psTr1 = c1ctx.enter_context(
                    nc.psum_tensor("psTr1", [128, 2048], BF16))
                wkr_bf = csb("wkr_bf", [128, 16 * 1024], BF16)
                t_wkr = sch.dma("gpsimd", "dwkr", lambda e: e.dma_start(
                    wkr_bf.rearrange("p (kc f) -> p kc f", f=1024),
                    d_wkr.rearrange("(kc p) f -> p kc f", p=128)))
                xn_fm1 = csb("xn_fm1", [128, 16 * 512], BF16)
                xc1 = [csb(f"xc1_{i}", [128, 2048], F32) for i in range(2)]
                krope_tm = [csb(f"krope_tm{i}", [128, 1024], BF16)
                            for i in range(4)]
                krope_fm = csb("krope_fm", [128, 8 * 512], BF16)
                outst = [csb("outst0", [128, 2048], F32)]
                ropeT1 = csb("ropeT1", [128, 1024], F32)


                r_x1 = [Res() for _ in range(2)]
                r_xnfm1 = Res()
                r_kr = [Res() for _ in range(4)]
                r_krfm = Res()
                r_out = [Res()]
                r_psC = Res()
                r_psC4 = [Res() for _ in range(4)]
                r_psTr1 = [Res(), Res()]

                kr_st = [csb(f"kr_st{i}", [128, 1024], F32) for i in range(2)]
                xn_tm1_4 = [csb(f"xn_tm1_4_{i}", [128, 2048], BF16)
                            for i in range(4)]
                r_krst = [Res(), Res()]
                r_xntm1_4 = [Res() for _ in range(4)]
                head_tn1 = {}
                norm_tok1 = {}

                def head_C1(tile):
                    t0 = tile * TC
                    tns = []
                    for kc in range(4):
                        xi = kc % 2
                        row = t0 + kc * 128
                        idx = tile * 4 + kc
                        t_xa = sch.dma("sync", f"dx{xi}",
                                       lambda e, r=row, i=xi: e.dma_start(
                                           xc1[i][:, 0:1024],
                                           d_x[r:r + 128, 0:1024]),
                                       waits=r_x1[xi].for_write())
                        t_xb = sch.dma("sync", f"dx{xi + 2}",
                                       lambda e, r=row, i=xi: e.dma_start(
                                           xc1[i][:, 1024:2048],
                                           d_x[r:r + 128, 1024:2048]),
                                       waits=r_x1[xi].for_write())
                        r_x1[xi].wrote(t_xb)
                        t_n = sch.op("gpsimd", lambda e, i=xi, kc=kc, idx=idx:
                                     e.tensor_scalar(
                                         xn_tm1_4[kc][:, :], xc1[i][:, :],
                                         st_rstd[:, idx:idx + 1],
                                         st_nmr[:, idx:idx + 1],
                                         ALU.mult, ALU.add),
                                     inc="sg",
                                     waits=[t_xa, t_xb]
                                     + r_xntm1_4[kc].for_write())
                        r_x1[xi].read(t_n)
                        r_xntm1_4[kc].wrote(t_n)
                        tns.append(t_n)
                    norm_tok1[tile] = tns

                xn_toks1 = {}

                def tr_C1(tile):
                    for kc in range(4):
                        ni = kc
                        t_n = norm_tok1[tile][kc]
                        for g in range(2):
                            tp = (kc * 2 + g) % 2
                            tt = None
                            for f8 in range(8):
                                fc = g * 8 + f8
                                tt = sch.op(
                                    "tensor",
                                    lambda e, ni=ni, fc=fc, f8=f8, tp=tp:
                                    e.transpose(
                                        psTr1[:, tp * 1024 + f8 * 128:
                                              tp * 1024 + (f8 + 1) * 128],
                                        xn_tm1_4[ni][:, fc * 128:(fc + 1) * 128],
                                        identbf[:, :]),
                                    inc="sp" if f8 == 7 else None,
                                    waits=([t_n] + r_psTr1[tp].for_write())
                                    if f8 == 0 else [])
                            r_psTr1[tp].wrote(tt)
                            r_xntm1_4[ni].read(tt)
                            dst = xn_fm1.rearrange("p (fc t) -> p fc t", t=512)[
                                :, g * 8:(g + 1) * 8, kc * 128:(kc + 1) * 128]
                            src_ = psTr1[:, tp * 1024:(tp + 1) * 1024] \
                                .rearrange("p (b t) -> p b t", t=128)
                            t_cp = sch.op("scalar", lambda e, d=dst, s=src_:
                                          e.activation(d, s, AF.Copy), inc="sa",
                                          waits=[tt] + r_xnfm1.for_write())
                            r_psTr1[tp].read(t_cp)
                    r_xnfm1.wrote(sch.cnt("sa"))
                    xn_toks1[tile] = sch.cnt("sa")

                def body_C1(tile):
                    t0 = tile * TC
                    xn_ready = xn_toks1[tile]

                    kr_done = {}
                    for half in range(2):
                        tt = None
                        for kk in range(16):
                            for tc2 in range(2):
                                tcl = half * 2 + tc2
                                for seg in range(2):
                                    w0 = []
                                    if kk == 0:
                                        w0 = r_psC4[tc2 * 2 + seg].for_write()
                                        if tc2 == 0 and seg == 0:
                                            w0 = w0 + [xn_ready, t_wkr]
                                    tt = sch.op(
                                        "tensor",
                                        lambda e, kk=kk, t=tcl, tc2=tc2, s=seg:
                                        e.matmul(
                                            psC[:, tc2 * 1024 + s * 512:
                                                tc2 * 1024 + (s + 1) * 512],
                                            xn_fm1[:, kk * 512 + t * 128:
                                                   kk * 512 + (t + 1) * 128],
                                            wkr_bf[:, kk * 1024 + s * 512:
                                                   kk * 1024 + (s + 1) * 512],
                                            start=(kk == 0), stop=(kk == 15)),
                                        inc="sp" if (kk == 15 and tc2 == 1
                                                     and seg == 1) else None,
                                        waits=w0)
                        for tc2 in range(2):
                            for seg in range(2):
                                r_psC4[tc2 * 2 + seg].wrote(tt)
                        for tc2 in range(2):
                            tcl = half * 2 + tc2
                            bc = tile * 4 + tcl
                            ki = tcl
                            si2 = tcl % 2
                            t_ke = sch.op("scalar", lambda e, tc2=tc2, i=si2:
                                          e.activation(
                                              kr_st[i][:, :],
                                              psC[:, tc2 * 1024:
                                                  (tc2 + 1) * 1024],
                                              AF.Copy), inc="sa",
                                          waits=[tt] + r_krst[si2].for_write())
                            r_psC4[tc2 * 2].read(t_ke)
                            r_psC4[tc2 * 2 + 1].read(t_ke)
                            r_krst[si2].wrote(t_ke)
                            kr = kr_st[si2].rearrange("p (h d) -> p h d", d=64)
                            kd = krope_tm[ki].rearrange("p (h d) -> p h d", d=64)
                            c16 = cosA[:, bc * 16:bc * 16 + 16].unsqueeze(1) \
                                .broadcast_to([128, 16, 16])
                            s16 = sinA[:, bc * 16:bc * 16 + 16].unsqueeze(1) \
                                .broadcast_to([128, 16, 16])
                            rT = [ropeT1.rearrange("p (s t) -> p s t", t=256)
                                  [:, i, :].rearrange("p (h j) -> p h j", j=16)
                                  for i in range(4)]
                            w0 = [t_ke] + r_kr[ki].for_write()
                            sch.op("vector", lambda e, a=rT[0], k=kr, c=c16:
                                   e.tensor_tensor(a, k[:, :, 0:16], c, ALU.mult),
                                   inc="sv", waits=w0)
                            sch.op("vector", lambda e, a=rT[1], k=kr, s=s16:
                                   e.tensor_tensor(a, k[:, :, 16:32], s,
                                                   ALU.mult), inc="sv")
                            sch.op("vector", lambda e, a=rT[0], b2=rT[1], d=kd:
                                   e.tensor_sub(d[:, :, 0:16], a, b2), inc="sv")
                            sch.op("vector", lambda e, a=rT[2], k=kr, c=c16:
                                   e.tensor_tensor(a, k[:, :, 16:32], c,
                                                   ALU.mult), inc="sv")
                            sch.op("vector", lambda e, a=rT[3], k=kr, s=s16:
                                   e.tensor_tensor(a, k[:, :, 0:16], s,
                                                   ALU.mult), inc="sv")
                            sch.op("vector", lambda e, a=rT[2], b2=rT[3], d=kd:
                                   e.tensor_add(d[:, :, 16:32], a, b2), inc="sv")
                            t_b = sch.op("vector", lambda e, k=kr, d=kd:
                                         e.tensor_copy(d[:, :, 32:64],
                                                       k[:, :, 32:64]), inc="sv")
                            r_krst[si2].read(t_b)
                            r_kr[ki].wrote(t_b)
                            kr_done[tcl] = sch.cnt("sv")
                    r_xnfm1.read(sch.cnt("sp"))
                    if tile + 1 < NT_C:
                        tr_C1(tile + 1)
                    for tcl in range(4):
                        ki = tcl
                        tp = tcl % 2
                        tt2 = None
                        for jc in range(8):
                            tt2 = sch.op(
                                "tensor",
                                lambda e, i=ki, jc=jc, tp=tp: e.transpose(
                                    psTr1[:, tp * 1024 + jc * 128:
                                          tp * 1024 + (jc + 1) * 128],
                                    krope_tm[i][:, jc * 128:(jc + 1) * 128],
                                    identbf[:, :]),
                                inc="sp" if jc == 7 else None,
                                waits=([kr_done[tcl]] + r_psTr1[tp].for_write())
                                if jc == 0 else [])
                        r_psTr1[tp].wrote(tt2)
                        r_kr[ki].read(tt2)
                        dst = krope_fm.rearrange("p (jc t) -> p jc t",
                                                 t=512)[
                            :, :, tcl * 128:(tcl + 1) * 128]
                        src = psTr1[:, tp * 1024:(tp + 1) * 1024] \
                            .rearrange("p (b t) -> p b t", t=128)
                        t_cp = sch.op("scalar", lambda e, d=dst, s=src:
                                      e.activation(d, s, AF.Copy), inc="sa",
                                      waits=[tt2] + r_krfm.for_write())
                        r_psTr1[tp].read(t_cp)
                    r_krfm.wrote(sch.cnt("sa"))
                    krope_ready = sch.cnt("sa")

                    for tcl in range(4):
                        oi = 0
                        xi = tcl % 2
                        row = t0 + tcl * 128
                        t_xa = sch.dma("sync", f"dx{xi}",
                                       lambda e, r=row, i=xi: e.dma_start(
                                           xc1[i][:, 0:1024],
                                           d_x[r:r + 128, 0:1024]),
                                       waits=r_x1[xi].for_write())
                        t_x = sch.dma("sync", f"dx{xi + 2}",
                                      lambda e, r=row, i=xi: e.dma_start(
                                          xc1[i][:, 1024:2048],
                                          d_x[r:r + 128, 1024:2048]),
                                      waits=r_x1[xi].for_write())
                        r_x1[xi].wrote(t_x)
                        tt = None
                        seg_tok = {}
                        for ci in range(12):
                            if ci < 4:
                                l_ap = ckv_fm[:, ci * OWN + t0 + tcl * 128:
                                              ci * OWN + t0 + (tcl + 1) * 128]
                                rbase, roff = w2_sb, ci * 2048
                            else:
                                jc = ci - 4
                                l_ap = krope_fm[:, jc * 512 + tcl * 128:
                                                jc * 512 + (tcl + 1) * 128]
                                rbase, roff = wrope_sb, jc * 2048
                            for seg in range(4):
                                w0 = []
                                if ci == 0:
                                    w0 = r_psC4[seg].for_write()
                                    if seg == 0:
                                        w0 = w0 + [krope_ready]
                                tt = sch.op(
                                    "tensor",
                                    lambda e, l=l_ap, rb=rbase, ro=roff, s=seg,
                                    ci=ci: e.matmul(
                                        psC[:, s * 512:(s + 1) * 512], l,
                                        rb[:, ro + s * 512:ro + (s + 1) * 512],
                                        start=(ci == 0), stop=(ci == 11)),
                                    inc="sp" if ci == 11 else None, waits=w0)
                                if ci == 11:
                                    seg_tok[seg] = tt
                        t_f = None
                        for seg in range(4):
                            r_psC4[seg].wrote(seg_tok[seg])
                            t_f = sch.op("vector",
                                         lambda e, i=oi, x=xi, s=seg:
                                         e.tensor_add(
                                             outst[i][:, s * 512:(s + 1) * 512],
                                             xc1[x][:, s * 512:(s + 1) * 512],
                                             psC[:, s * 512:(s + 1) * 512]),
                                         inc="sv",
                                         waits=[seg_tok[seg], t_xa, t_x]
                                         + r_out[oi].for_write())
                            r_psC4[seg].read(t_f)
                        r_x1[xi].read(t_f)
                        r_out[oi].wrote(t_f)
                        t_y = sch.dma("sync", "dy0",
                                      lambda e, r=row, i=oi: e.dma_start(
                                          d_y[r:r + 128, :], outst[i][:, :]),
                                      waits=[t_f])
                        r_out[oi].read(t_y)
                    r_krfm.read(sch.cnt("sp"))

                head_C1(0)
                tr_C1(0)
                head_C1(1)
                for tile in range(NT_C):
                    body_C1(tile)
                    if tile + 2 < NT_C:
                        head_C1(tile + 2)

                sch.op("sync", lambda e: None,
                       waits=[sch.cnt("dy0"), sch.cnt("dy1")])
                sch.flush(nc, semmap)

        abc1.close()
        if phases <= 4:
            return nc
        with ExitStack() as c2ctx:
            csb = lambda n, s, d: c2ctx.enter_context(nc.sbuf_tensor(n, s, d))
            xn_fm2 = csb("xn_fm2", [128, 16 * 512], BF16)
            xn_tm2 = [csb(f"xn_tm2_{i}", [128, 2048], BF16) for i in range(4)]
            f32b = [csb(f"f32b{i}", [128, 2048], F32) for i in range(2)]
            outst2p = [csb(f"outst2_{i}", [128, 2048], F32) for i in range(2)]
            slab = [csb(f"slab{i}", [128, 16 * 512], BF16) for i in range(2)]
            wdn = [csb(f"wdn{i}", [128, 2 * 2048], BF16) for i in range(4)]
            fuse = csb("fuse", [128, 64 * 512], BF16)
            gsil = [csb(f"gsil{i}", [128, 512], BF16) for i in range(4)]

            r_x2 = [Res() for _ in range(2)]
            r_xntm2 = [Res() for _ in range(4)]
            r_xnfm2 = Res()
            r_out2 = [Res(), Res()]
            r_slab = [Res() for _ in range(2)]
            r_wdn = [Res() for _ in range(4)]
            r_fuse = Res()
            r_gsil = [Res() for _ in range(4)]
            r_psDh = [Res(), Res()]
            r_psTr2 = [Res(), Res()]
            r_psG = [Res(), Res()]
            r_psU = [Res(), Res()]
            psD = c2ctx.enter_context(
                nc.psum_tensor("psD", [128, 4096], F32))
            psG = psD[:, 0:1024]
            psU = psD[:, 1024:2048]
            psTr2 = psD[:, 2048:3072].bitcast(BF16)

            head_tn2 = {}
            norm_tok2 = {}
            pf_slab = {}

            t_idf = sch.dma("sync", "dI", lambda e: e.dma_start(
                identf[:, :], d_ident[:, :]))
            t_sgN = sch.dma("sync", "dI", lambda e: e.dma_start(
                f32b[1][0:64, 128:256], d_sg.rearrange("(a p) -> a p", p=128)))
            t_suN = sch.dma("sync", "dI", lambda e: e.dma_start(
                f32b[1][0:64, 256:384], d_su.rearrange("(a p) -> a p", p=128)))
            t_sdr = sch.dma("sync", "dI", lambda e: e.dma_start(
                f32b[0][0:1, :], d_sd.rearrange("(a f) -> a f", a=1)))
            t_ones = sch.op("gpsimd", lambda e: e.memset(
                f32b[1][0:1, 0:128], 1.0), inc="sg")
            t_tr1 = sch.op("tensor", lambda e: e.transpose(
                psD[:, 0:64], f32b[1][0:64, 128:256], identf[0:64, 0:64]),
                inc="sp", waits=[t_sgN, t_idf])
            t_c1 = sch.op("vector", lambda e: e.tensor_copy(
                sg_col[:, :], psD[:, 0:64]), inc="sv", waits=[t_tr1])
            t_tr2 = sch.op("tensor", lambda e: e.transpose(
                psD[:, 64:128], f32b[1][0:64, 256:384], identf[0:64, 0:64]),
                inc="sp", waits=[t_suN])
            t_c2i = sch.op("vector", lambda e: e.tensor_copy(
                su_col[:, :], psD[:, 64:128]), inc="sv", waits=[t_tr2])
            r_psG[0].wrote(t_tr2)
            r_psG[0].read(t_c1)
            r_psG[0].read(t_c2i)
            sd_last = {0: t_tr2, 1: None}
            for seg in range(4):
                sl = seg % 2
                w0 = [t_ones, t_sdr] + r_psG[sl].for_write()
                tt = sch.op("tensor", lambda e, s=seg, sl=sl: e.matmul(
                    psD[:, sl * 512:(sl + 1) * 512], f32b[1][0:1, 0:128],
                    f32b[0][0:1, s * 512:(s + 1) * 512],
                    start=True, stop=True), inc="sp", waits=w0)
                r_psG[sl].wrote(tt)
                tc_ = sch.op("vector", lambda e, s=seg, sl=sl: e.tensor_copy(
                    sdown_rep[:, s * 512:(s + 1) * 512],
                    psD[:, sl * 512:(sl + 1) * 512]), inc="sv", waits=[tt])
                r_psG[sl].read(tc_)
            r_x2[0].wrote(t_sdr)
            r_x2[1].wrote(t_suN)
            for sl in range(2):
                r_x2[0].read(r_psG[sl].w)
                r_x2[1].read(r_psG[sl].w)
            init2_sg, init2_su = t_c1, t_c2i

            def head_C2(tile):
                t0 = tile * TC
                tns = []
                for kc in range(4):
                    xi = kc % 2
                    row = t0 + kc * 128
                    idx = tile * 4 + kc
                    t_xa = sch.dma("sync", f"dx{xi}",
                                   lambda e, r=row, i=xi: e.dma_start(
                                       f32b[i][:, 0:1024],
                                       d_x[r:r + 128, 0:1024]),
                                   waits=r_x2[xi].for_write())
                    t_xb = sch.dma("sync", f"dx{xi + 2}",
                                   lambda e, r=row, i=xi: e.dma_start(
                                       f32b[i][:, 1024:2048],
                                       d_x[r:r + 128, 1024:2048]),
                                   waits=r_x2[xi].for_write())
                    r_x2[xi].wrote(t_xb)
                    t_n = sch.op("gpsimd", lambda e, i=xi, kc=kc, idx=idx:
                                 e.tensor_scalar(
                                     xn_tm2[kc][:, :], f32b[i][:, :],
                                     st_rstd[:, idx:idx + 1],
                                     st_nmr[:, idx:idx + 1],
                                     ALU.mult, ALU.add),
                                 inc="sg",
                                 waits=[t_xa, t_xb]
                                 + r_xntm2[kc].for_write())
                    r_x2[xi].read(t_n)
                    r_xntm2[kc].wrote(t_n)
                    tns.append(t_n)
                norm_tok2[tile] = tns

            xn_toks2 = {}

            def tr_C2(tile):
                for kc in range(4):
                    ni = kc
                    t_n = norm_tok2[tile][kc]
                    for g in range(2):
                        tp = (kc * 2 + g) % 2
                        tt = None
                        for f8 in range(8):
                            fc = g * 8 + f8
                            tt = sch.op(
                                "tensor",
                                lambda e, ni=ni, fc=fc, f8=f8, tp=tp:
                                e.transpose(
                                    psTr2[:, tp * 1024 + f8 * 128:
                                          tp * 1024 + (f8 + 1) * 128],
                                    xn_tm2[ni][:, fc * 128:(fc + 1) * 128],
                                    identbf[:, :]),
                                inc="sp" if f8 == 7 else None,
                                waits=([t_n]
                                       + r_psTr2[tp].for_write()
                                       + r_psDh[1].for_write())
                                if f8 == 0 else [])
                        r_psTr2[tp].wrote(tt)
                        r_xntm2[ni].read(tt)
                        dst = xn_fm2.rearrange("p (fc t) -> p fc t", t=512)[
                            :, g * 8:(g + 1) * 8, kc * 128:(kc + 1) * 128]
                        src_ = psTr2[:, tp * 1024:(tp + 1) * 1024] \
                            .rearrange("p (b t) -> p b t", t=128)
                        t_cp = sch.op("scalar", lambda e, d=dst, s=src_:
                                      e.activation(d, s, AF.Copy), inc="sa",
                                      waits=[tt] + r_xnfm2.for_write())
                        r_psTr2[tp].read(t_cp)
                r_xnfm2.wrote(sch.cnt("sa"))
                xn_toks2[tile] = sch.cnt("sa")

            def body_C2(tile):
                t0 = tile * TC
                xn_ready = xn_toks2[tile]

                for iseg in range(16):
                    c0 = iseg * 512
                    if iseg == 0 and tile in pf_slab:
                        t_lg, t_lu = pf_slab.pop(tile)
                    else:
                        t_lg = sch.dma("gpsimd", "dsl0", lambda e, c0=c0:
                                       e.dma_start(
                                           slab[0].rearrange(
                                               "p (kc i) -> p kc i", i=512),
                                           d_wg[:, c0:c0 + 512].rearrange(
                                               "(kc p) i -> p kc i", p=128)),
                                       waits=r_slab[0].for_write())
                        r_slab[0].wrote(t_lg)
                        t_lu = sch.dma("gpsimd", "dsl1", lambda e, c0=c0:
                                       e.dma_start(
                                           slab[1].rearrange(
                                               "p (kc i) -> p kc i", i=512),
                                           d_wu[:, c0:c0 + 512].rearrange(
                                               "(kc p) i -> p kc i", p=128)),
                                       waits=r_slab[1].for_write())
                        r_slab[1].wrote(t_lu)
                    sil_tok = {}
                    for icl in range(4):
                        ic = iseg * 4 + icl
                        gp = ic % 2
                        tt = None
                        for kk in range(16):
                            w0 = []
                            if kk == 0:
                                w0 = ([xn_ready, t_lg] + r_psG[gp].for_write()
                                      + r_psDh[0].for_write())
                            tt = sch.op(
                                "tensor",
                                lambda e, kk=kk, icl=icl, gp=gp:
                                e.matmul(
                                    psG[:, gp * 512:(gp + 1) * 512],
                                    slab[0][:, kk * 512 + icl * 128:
                                            kk * 512 + (icl + 1) * 128],
                                    xn_fm2[:, kk * 512:(kk + 1) * 512],
                                    start=(kk == 0), stop=(kk == 15)),
                                inc="sp" if kk == 15 else None, waits=w0)
                        r_psG[gp].wrote(tt)
                        r_slab[0].read(tt)
                        t_sil = sch.op(
                            "scalar", lambda e, ic=ic, gp=gp, icl=icl:
                            e.activation(
                                gsil[icl][:, :],
                                psG[:, gp * 512:(gp + 1) * 512], AF.Silu,
                                scale=sg_col[:, ic:ic + 1]),
                            inc="sa", waits=[tt] + r_gsil[icl].for_write()
                            + ([init2_sg] if ic == 0 and tile == 0 else []))
                        r_psG[gp].read(t_sil)
                        r_gsil[icl].wrote(t_sil)
                        sil_tok[icl] = t_sil
                    for icl in range(4):
                        ic = iseg * 4 + icl
                        up = icl % 2
                        tt = None
                        for kk in range(16):
                            w0 = []
                            if kk == 0:
                                w0 = ([t_lu] + r_psU[up].for_write()
                                      + r_psDh[0].for_write())
                            tt = sch.op(
                                "tensor",
                                lambda e, kk=kk, icl=icl, up=up: e.matmul(
                                    psU[:, up * 512:(up + 1) * 512],
                                    slab[1][:, kk * 512 + icl * 128:
                                            kk * 512 + (icl + 1) * 128],
                                    xn_fm2[:, kk * 512:(kk + 1) * 512],
                                    start=(kk == 0), stop=(kk == 15)),
                                inc="sp" if kk == 15 else None, waits=w0)
                        r_psU[up].wrote(tt)
                        r_slab[1].read(tt)
                        t_f = sch.op(
                            "vector", lambda e, ic=ic, icl=icl, up=up:
                            e.scalar_tensor_tensor(
                                fuse[:, ic * 512:(ic + 1) * 512],
                                psU[:, up * 512:(up + 1) * 512],
                                su_col[:, ic:ic + 1],
                                gsil[icl][:, :], ALU.mult, ALU.mult),
                            inc="sv",
                            waits=([tt, sil_tok[icl]] + r_fuse.for_write())
                            if ic == 0 else [tt, sil_tok[icl]])
                        r_psU[up].read(t_f)
                        r_gsil[icl].read(t_f)
                r_xnfm2.read(sch.cnt("sp"))
                r_fuse.wrote(sch.cnt("sv"))
                fuse_ready = sch.cnt("sv")

                if tile + 1 < NT_C:
                    tr_C2(tile + 1)
                sa_prev = sch.cnt("sa")

                if tile + 1 < NT_C:
                    t_pg = sch.dma("gpsimd", "dsl0", lambda e: e.dma_start(
                        slab[0].rearrange("p (kc i) -> p kc i", i=512),
                        d_wg[:, 0:512].rearrange("(kc p) i -> p kc i", p=128)),
                        waits=r_slab[0].for_write())
                    r_slab[0].wrote(t_pg)
                    t_pu = sch.dma("gpsimd", "dsl1", lambda e: e.dma_start(
                        slab[1].rearrange("p (kc i) -> p kc i", i=512),
                        d_wu[:, 0:512].rearrange("(kc p) i -> p kc i", p=128)),
                        waits=r_slab[1].for_write())
                    r_slab[1].wrote(t_pu)
                    pf_slab[tile + 1] = (t_pg, t_pu)

                sring = ["dwd0", "dwd1", "dwd2", "dwd3"]
                ld_tok = {}

                def wdn_load(j):
                    di = j % 4
                    r0 = (j % 32) * 256
                    t = sch.dma(
                        "gpsimd", sring[di],
                        lambda e, d=di, r0=r0: e.dma_start(
                            wdn[d].rearrange("p (c f) -> p c f", f=2048),
                            d_wd[r0:r0 + 256, :].rearrange(
                                "(c p) f -> p c f", p=128)),
                        waits=r_wdn[di].for_write())
                    r_wdn[di].wrote(t)
                    ld_tok[j] = t

                for j in range(3):
                    wdn_load(j)
                for half in range(2):
                    yp_tok = {}
                    for tc2 in range(2):
                        tcl = half * 2 + tc2
                        row = t0 + tcl * 128
                        yi = tcl % 2
                        t_yp = sch.dma("sync", f"dx{yi}",
                                       lambda e, r=row, i=yi: e.dma_start(
                                           f32b[i][:, :], d_y[r:r + 128, :]),
                                       waits=r_x2[yi].for_write())
                        r_x2[yi].wrote(t_yp)
                        yp_tok[tc2] = t_yp
                    tt = None
                    for jl in range(32):
                        jj = half * 32 + jl
                        di = jj % 4
                        t_ld = ld_tok[jj]
                        for c2 in range(2):
                            ic = jl * 2 + c2
                            for tc2 in range(2):
                                tcl = half * 2 + tc2
                                for seg in range(4):
                                    w0 = []
                                    if ic == 0 and seg == 0:
                                        w0 = ([fuse_ready, t_ld, sa_prev]
                                              + r_psDh[tc2].for_write())
                                    elif c2 == 0 and tc2 == 0 and seg == 0:
                                        w0 = [t_ld]
                                    tt = sch.op(
                                        "tensor",
                                        lambda e, d=di, ic=ic, c2=c2, t=tcl,
                                        tc2=tc2, s=seg: e.matmul(
                                            psD[:, tc2 * 2048 + s * 512:
                                                tc2 * 2048 + (s + 1) * 512],
                                            fuse[:, ic * 512 + t * 128:
                                                 ic * 512 + (t + 1) * 128],
                                            wdn[d][:, c2 * 2048 + s * 512:
                                                   c2 * 2048 +
                                                   (s + 1) * 512],
                                            start=(ic == 0),
                                            stop=(ic == 63)),
                                        inc="sp" if (c2 == 1 and tc2 == 1
                                                     and seg == 3)
                                        else None, waits=w0)
                        r_wdn[di].read(tt)
                        if jj + 3 < 64:
                            wdn_load(jj + 3)
                    r_psDh[0].wrote(tt)
                    r_psDh[1].wrote(tt)
                    r_fuse.read(tt)
                    for tc2 in (1, 0):
                        tcl = half * 2 + tc2
                        row = t0 + tcl * 128
                        yi = tcl % 2
                        t_m = sch.op("vector", lambda e, tc2=tc2:
                                     e.tensor_mul(
                                         outst2p[tc2][:, :],
                                         psD[:, tc2 * 2048:(tc2 + 1) * 2048],
                                         sdown_rep[:, :]), inc="sv",
                                     waits=r_psDh[tc2].for_read()
                                     + r_out2[tc2].for_write())
                        r_out2[tc2].wrote(t_m)
                        r_psDh[tc2].read(t_m)
                        t_o = sch.op("vector", lambda e, i=yi, tc2=tc2:
                                     e.tensor_add(
                                         outst2p[tc2][:, :],
                                         outst2p[tc2][:, :], f32b[i][:, :]),
                                     inc="sv", waits=[yp_tok[tc2]])
                        r_x2[yi].read(t_o)
                        r_out2[tc2].wrote(t_o)
                        t_y = sch.dma("sync", f"dy{tc2}",
                                      lambda e, r=row, tc2=tc2: e.dma_start(
                                          d_y[r:r + 128, :],
                                          outst2p[tc2][:, :]),
                                      waits=[t_o])
                        r_out2[tc2].read(t_y)

            head_C2(0)
            tr_C2(0)
            head_C2(1)
            for tile in range(NT_C):
                body_C2(tile)
                if tile + 2 < NT_C:
                    head_C2(tile + 2)

            sch.op("sync", lambda e: None,
                   waits=[sch.cnt("dy0"), sch.cnt("dy1")])
            sch.flush(nc, semmap)

    return nc


_nc_cache = None


def kernel(L0_attn_x, L0_attn_W_dkv, L0_attn_W_dq, L0_attn_W_uk, L0_attn_W_uv,
           L0_attn_W_uq, L0_attn_W_kr, L0_attn_W_qr, L0_attn_W_out,
           L0_ffn_w_gate, L0_ffn_w_up, L0_ffn_w_down, L0_ffn_s_gate,
           L0_ffn_s_up, L0_ffn_s_down):
    global _nc_cache
    x = np.asarray(L0_attn_x, dtype=np.float32)
    weights = dict(
        w_dkv=np.ascontiguousarray(L0_attn_W_dkv, np.float32),
        w_dq=np.ascontiguousarray(L0_attn_W_dq, np.float32),
        w_uk=np.ascontiguousarray(L0_attn_W_uk, np.float32),
        w_uv=np.ascontiguousarray(L0_attn_W_uv, np.float32),
        w_uq=np.ascontiguousarray(L0_attn_W_uq, np.float32),
        w_kr=np.ascontiguousarray(L0_attn_W_kr, np.float32),
        w_qr=np.ascontiguousarray(L0_attn_W_qr, np.float32),
        w_out=np.ascontiguousarray(L0_attn_W_out, np.float32),
        w_gate=np.ascontiguousarray(L0_ffn_w_gate, np.int8),
        w_up=np.ascontiguousarray(L0_ffn_w_up, np.int8),
        w_down=np.ascontiguousarray(L0_ffn_w_down, np.int8),
        s_gate=np.ascontiguousarray(L0_ffn_s_gate, np.float32),
        s_up=np.ascontiguousarray(L0_ffn_s_up, np.float32),
        s_down=np.ascontiguousarray(L0_ffn_s_down, np.float32),
        ident=np.eye(128, dtype=np.float32),
    )
    if _nc_cache is None:
        _nc_cache = build_program()
    nc = _nc_cache

    in_maps = []
    for core in range(8):
        b, half = core // 2, core % 2
        xb = x[b]
        own = xb[half * OWN:(half + 1) * OWN]
        other = xb[(1 - half) * OWN:(2 - half) * OWN]
        m = dict(weights)
        m["xb"] = np.ascontiguousarray(np.concatenate([own, other], axis=0))
        m["posb"] = np.array([[half * OWN, (1 - half) * OWN]], dtype=np.float32)
        in_maps.append(m)

    res = run_bass_kernel_spmd(nc, in_maps, core_ids=list(range(8)))
    out = np.empty((B, S, H), dtype=np.float32)
    for core in range(8):
        b, half = core // 2, core % 2
        out[b, half * OWN:(half + 1) * OWN, :] = res.results[core]["y"]
    return out
